# version 02
# baseline (speedup 1.0000x reference)
import numpy as np
import ml_dtypes
import concourse.bass as bass
import concourse.mybir as mybir
from concourse.bass_utils import run_bass_kernel_spmd
from contextlib import ExitStack

F32 = mybir.dt.float32
BF16 = mybir.dt.bfloat16
AF = mybir.ActivationFunctionType
ALU = mybir.AluOpType
AX = mybir.AxisListType

B, T, D = 8, 4096, 1024
DFF, NE = 4096, 8
CAP = 5120.0
NEGM = -1.0e30
EPS = 1e-5
ISC = 1.0 / 32.0
NITER = 30
BLKS = [384] * 10 + [256]
NCORE = 8


class S(object):
    def __init__(self, nc, es, name):
        self.h = es.enter_context(nc.semaphore(name))
        self.n = 0

    def inc(self, inst, k=1):
        inst.then_inc(self.h, k)
        self.n += k
        return self.n


class BankGate(object):

    def __init__(self):
        self.need = {}

    def wait(self, eng, bank):
        for sem, cnt in self.need.get(bank, []):
            eng.wait_ge(sem.h, cnt)

    def set(self, bank, *pairs):
        self.need[bank] = [(s, s.n) for s in pairs]


def _bf(a):
    return np.ascontiguousarray(np.asarray(a)).astype(ml_dtypes.bfloat16)


def host_consts():
    c = {}
    c["idn_bf"] = _bf(np.eye(128, dtype=np.float32))
    ones128 = np.ones((128, 128), np.float32)
    c["ones2"] = ones128
    c["ones_f"] = np.ones((1, 128), np.float32)
    c["ones_b"] = _bf(np.ones((1, 128), np.float32))
    c["ones_c"] = np.ones((128, 1), np.float32)
    hs = np.zeros((128, 32), np.float32)
    for p in range(128):
        for tt in range(4):
            hs[p, tt * 8 + 2 * tt + p // 64] = 1.0
    c["hs_bf"] = _bf(hs)
    sel = np.zeros((8, 512), np.float32)
    for p in range(128):
        for tt in range(4):
            sel[2 * tt + p // 64, tt * 128 + p] = 1.0
    c["sel_bf"] = _bf(sel)
    tri = np.zeros((128, 128), np.float32)
    for j in range(128):
        tri[j, :j] = NEGM
    c["tri"] = tri
    c["iota8"] = np.ascontiguousarray(
        np.broadcast_to(np.arange(8, dtype=np.float32), (128, 8)))
    return c


def build_l1(debug=False):
    nc = bass.Bass("TRN2", target_bir_lowering=False, debug=False)
    dr = lambda n, s, d, k="ExternalInput": nc.dram_tensor(n, s, d, kind=k).ap()
    xT = dr("xT", [D, T], F32)
    x_in = dr("x", [T, D], F32)
    wT = {w: dr(w + "T", [D, D], F32) for w in ("wq", "wk", "wv", "wo")}
    rwT = dr("rwT", [D, NE], F32)
    bo_row = dr("bo_row", [1, D], F32)
    cols_d = {n: dr(n, [128, 1], F32) for n in ("qg_c", "qb_c", "kg_c", "kb_c")}
    idn_d = dr("idn_bf", [128, 128], BF16)
    ones2_d = dr("ones2", [128, 128], F32)
    hs_d = dr("hs_bf", [128, 32], BF16)
    sel_d = dr("sel_bf", [8, 512], BF16)
    tri_d = dr("tri", [128, 128], F32)
    io8_d = dr("iota8", [128, 8], F32)
    x1_out = dr("x1_out", [T, D], F32, "ExternalOutput")
    x1fmT_out = dr("x1fmT_out", [D, T], BF16, "ExternalOutput")
    route_out = dr("route_out", [2, T], F32, "ExternalOutput")
    aout_h = nc.dram_tensor("aout_h", [D, T], BF16,
                            kind="ExternalOutput" if debug else "Internal").ap()
    if debug:
        dbg_qh = dr("dbg_qh", [128, 2 * T], BF16, "ExternalOutput")
        dbg_kh = dr("dbg_kh", [128, 2 * T], BF16, "ExternalOutput")
        dbg_vx = dr("dbg_vx", [128, 32 * 4 * 65], BF16, "ExternalOutput")
        dbg_nb = dr("dbg_nb", [128, 4], F32, "ExternalOutput")
        dbg_mx = dr("dbg_mx", [128, 4], F32, "ExternalOutput")
        dbg_mxr = dr("dbg_mxr", [1, 8], F32, "ExternalOutput")
        dbg_nbr = dr("dbg_nbr", [1, 4], F32, "ExternalOutput")

    es = ExitStack()
    sb = lambda n, s, d: es.enter_context(nc.sbuf_tensor(n, s, d))
    xfm = sb("xfm", [128, 8 * T], BF16)
    qh = sb("qh", [128, 2 * T], BF16)
    kh = sb("kh", [128, 2 * T], BF16)
    vx = sb("vx", [128, 32 * 4 * 65], BF16)
    wst = sb("wst", [128, 2048], F32)
    wbf = sb("wbf", [128, 2048], BF16)
    qf32 = sb("qf32", [128, 2048], F32)
    qbf_s = sb("qbf_s", [128, 512], BF16)
    q2bf_s = sb("q2bf_s", [128, 512], BF16)
    stA = sb("stA", [8, 1024], F32)
    stR = sb("stR", [8, 512], BF16)
    stMR = sb("stMR", [8, 512], BF16)
    mx = sb("mx", [128, 4], F32)
    mxrow = sb("mxrow", [1, 8], F32)
    nbrow = sb("nbrow", [1, 4], F32)
    negB = sb("negB", [128, 4], F32)
    ap_s = sb("ap_s", [128, 512], F32)
    Pbuf = sb("Pbuf", [128, 2048], BF16)
    rd = sb("rd", [128, 512], F32)
    rb = sb("rb", [64, 512], F32)
    aout_s = sb("aout_s", [64, 1024], BF16)
    aord = sb("aord", [128, 2048], BF16)
    wo_bf = sb("wo_bf", [128, 8192], BF16)
    bo_b = sb("bo_b", [128, D], F32)
    sc = sb("sc", [128, 2048], F32)
    zt = sb("zt", [128, D], F32)
    x1f = sb("x1f", [128, D], F32)
    x1b = sb("x1b", [128, D], BF16)
    x1fmt = sb("x1fmt", [128, D], BF16)
    e8 = sb("e8", [128, 8], F32)
    oh8 = sb("oh8", [128, 8], F32)
    j8 = sb("j8", [128, 8], F32)
    lnst = sb("lnst", [128, 8], F32)
    route_p = sb("route_p", [128, 32], F32)
    route_i = sb("route_i", [128, 32], F32)
    rwst = sb("rwst", [128, 64], F32)
    rw_bf = sb("rw_bf", [128, 64], BF16)
    c_idn = sb("c_idn", [128, 128], BF16)
    c_one2 = sb("c_one2", [128, 128], F32)
    c_hs = sb("c_hs", [128, 32], BF16)
    c_sel = sb("c_sel", [8, 512], BF16)
    c_tri = sb("c_tri", [128, 128], F32)
    c_io8 = sb("c_io8", [128, 8], F32)
    c_col = {n: sb("c_" + n, [128, 1], F32) for n in ("qg_c", "qb_c", "kg_c", "kb_c")}
    cEPS = sb("cEPS", [128, 1], F32)
    c30 = sb("c30", [128, 1], F32)

    pb = [es.enter_context(nc.psum_tensor("pb%d" % i, [128, 512], F32))
          for i in range(6)]
    tps = [es.enter_context(nc.psum_tensor("tp%d" % i, [128, 1024], BF16))
           for i in range(2)]

    dS = S(nc, es, "dS")
    dG = S(nc, es, "dG")
    qP = S(nc, es, "qP")
    qV = S(nc, es, "qV")
    qA = S(nc, es, "qA")
    qG = S(nc, es, "qG")
    BG = BankGate()

    P, V, A, G, SY = nc.tensor, nc.vector, nc.scalar, nc.gpsimd, nc.sync

    with nc.Block():
        qV.inc(V.memset(cEPS[:], EPS))
        qV.inc(V.memset(c30[:], 1e-30))
        for dst, src in [(c_idn, idn_d), (c_one2, ones2_d), (c_hs, hs_d),
                         (c_sel, sel_d), (c_tri, tri_d), (c_io8, io8_d)]:
            dS.inc(SY.dma_start(out=dst[:], in_=src), 16)
        for n in cols_d:
            dS.inc(SY.dma_start(out=c_col[n][:], in_=cols_d[n]), 16)
        dS.inc(SY.dma_start(
            out=rwst[:].rearrange("p (a e) -> p a e", a=8),
            in_=rwT.rearrange("(a p) e -> p a e", p=128)), 16)
        for ec in range(8):
            for hh in range(2):
                if ec + hh > 0:
                    SY.wait_ge(qV.h, qV.n)
                dS.inc(SY.dma_start(
                    out=sc[:],
                    in_=xT[ec * 128:(ec + 1) * 128, hh * 2048:(hh + 1) * 2048]), 16)
                V.wait_ge(dS.h, dS.n)
                qV.inc(V.tensor_copy(
                    out=xfm[:, ec * T + hh * 2048: ec * T + (hh + 1) * 2048],
                    in_=sc[:]))
        qV.inc(V.tensor_copy(out=rw_bf[:], in_=rwst[:]))

        aout_dma = {0: 0, 1: 0}
        for g in range(4):
            Fg = 256 * g

            def load_w(which):
                SY.wait_ge(qV.h, qV.n)
                for ec in range(8):
                    dS.inc(SY.dma_start(
                        out=wst[:, ec * 256:(ec + 1) * 256],
                        in_=wT[which][ec * 128:(ec + 1) * 128, Fg:Fg + 256]), 16)
                V.wait_ge(dS.h, dS.n)
                V.wait_ge(qP.h, qP.n)
                qV.inc(V.tensor_copy(out=wbf[:], in_=wst[:]))
                return qV.n

            w_rdy = load_w("wv")
            G.wait_ge(qP.h, qP.n)
            qG.inc(G.memset(
                vx[:].rearrange("p (k h o) -> p k h o", k=32, h=4)[:, :, :, 64:65],
                1.0))
            for tch in range(32):
                bank = tch % 2
                P.wait_ge(qV.h, w_rdy)
                BG.wait(P, bank)
                for ec in range(8):
                    mm = P.matmul(
                        out=pb[bank][:, 0:256],
                        lhsT=xfm[:, ec * T + tch * 128: ec * T + (tch + 1) * 128],
                        rhs=wbf[:, ec * 256:(ec + 1) * 256],
                        start=(ec == 0), stop=(ec == 7))
                qP.inc(mm)
                V.wait_ge(qP.h, qP.n)
                qV.inc(V.tensor_copy(
                    out=vx[:, tch * 260:(tch + 1) * 260]
                        .rearrange("p (h o) -> p h o", h=4)[:, :, 0:64],
                    in_=pb[bank][:, 0:256].rearrange("p (h o) -> p h o", h=4)))
                BG.set(bank, qV)

            for which, dst_t, gcol, bcol in (
                    ("wq", qh, c_col["qg_c"], c_col["qb_c"]),
                    ("wk", kh, c_col["kg_c"], c_col["kb_c"])):
                w_rdy = load_w(which)
                for c in range(2):
                    for hc in range(2):
                        for i in range(4):
                            tt = hc * 4 + i
                            bank = i % 2
                            P.wait_ge(qV.h, w_rdy)
                            BG.wait(P, bank)
                            for ec in range(8):
                                mm = P.matmul(
                                    out=pb[bank][:],
                                    lhsT=wbf[:, ec * 256 + c * 128:
                                             ec * 256 + (c + 1) * 128],
                                    rhs=xfm[:, ec * T + tt * 512:
                                            ec * T + (tt + 1) * 512],
                                    start=(ec == 0), stop=(ec == 7))
                            qP.inc(mm)
                            V.wait_ge(qP.h, qP.n)
                            qV.inc(V.tensor_copy(out=qf32[:, i * 512:(i + 1) * 512],
                                                 in_=pb[bank][:]))
                            qV.inc(V.tensor_copy(out=qbf_s[:], in_=pb[bank][:]))
                            BG.set(bank, qV)
                            A.wait_ge(qV.h, qV.n)
                            qA.inc(A.activation(out=q2bf_s[:], in_=qbf_s[:],
                                                func=AF.Square))
                            BG.wait(P, "st")
                            P.wait_ge(qA.h, qA.n)
                            mm = P.matmul(out=pb[2][0:8, :],
                                          lhsT=c_hs[:, i * 8:(i + 1) * 8],
                                          rhs=qbf_s[:],
                                          start=(i == 0), stop=(i == 3))
                            mm = P.matmul(out=pb[3][0:8, :],
                                          lhsT=c_hs[:, i * 8:(i + 1) * 8],
                                          rhs=q2bf_s[:],
                                          start=(i == 0), stop=(i == 3))
                            qP.inc(mm)
                        A.wait_ge(qP.h, qP.n)
                        qA.inc(A.mul(out=stA[:, 0:512], in_=pb[2][0:8, :],
                                     mul=1.0 / 64))
                        V.wait_ge(qA.h, qA.n)
                        qV.inc(V.tensor_tensor(out=stA[:, 512:1024],
                                               in0=stA[:, 0:512],
                                               in1=stA[:, 0:512], op=ALU.mult))
                        qV.inc(V.scalar_tensor_tensor(
                            out=stA[:, 512:1024], in0=pb[3][0:8, :],
                            scalar=1.0 / 64, in1=stA[:, 512:1024],
                            op0=ALU.mult, op1=ALU.subtract))
                        A.wait_ge(qV.h, qV.n)
                        qA.inc(A.activation(out=stA[:, 512:1024],
                                            in_=stA[:, 512:1024],
                                            func=AF.Ln, bias=cEPS[0:8, 0:1]))
                        qA.inc(A.activation(out=stA[:, 512:1024],
                                            in_=stA[:, 512:1024],
                                            func=AF.Exp, scale=-0.5))
                        V.wait_ge(qA.h, qA.n)
                        qV.inc(V.tensor_copy(out=stR[:], in_=stA[:, 512:1024]))
                        qV.inc(V.tensor_tensor(out=stMR[:], in0=stA[:, 0:512],
                                               in1=stA[:, 512:1024], op=ALU.mult))
                        BG.set("st", qA, qV)
                        st_rdy = qV.n
                        for i in range(4):
                            tt = hc * 4 + i
                            P.wait_ge(qV.h, max(st_rdy, qV.n))
                            BG.wait(P, 4)
                            BG.wait(P, 5)
                            mm = P.matmul(out=pb[4][:],
                                          lhsT=c_sel[:, i * 128:(i + 1) * 128],
                                          rhs=stR[:], start=True, stop=True)
                            mm = P.matmul(out=pb[5][:],
                                          lhsT=c_sel[:, i * 128:(i + 1) * 128],
                                          rhs=stMR[:], start=True, stop=True)
                            qP.inc(mm)
                            V.wait_ge(qP.h, qP.n)
                            qV.inc(V.tensor_tensor(
                                out=ap_s[:], in0=qf32[:, i * 512:(i + 1) * 512],
                                in1=pb[4][:], op=ALU.mult))
                            qV.inc(V.tensor_tensor(out=ap_s[:], in0=ap_s[:],
                                                   in1=pb[5][:], op=ALU.subtract))
                            qV.inc(V.tensor_scalar(
                                out=dst_t[:, c * T + tt * 512:
                                          c * T + (tt + 1) * 512],
                                in0=ap_s[:], scalar1=gcol[:, 0:1],
                                scalar2=bcol[:, 0:1], op0=ALU.mult, op1=ALU.add))
                            BG.set(4, qV)
                            BG.set(5, qV)

            for c in range(2):
                qV.inc(V.tensor_reduce(out=mx[:, c:c + 1],
                                       in_=qh[:, c * T:(c + 1) * T], axis=AX.X,
                                       op=ALU.max, apply_absolute_value=True))
                qV.inc(V.tensor_reduce(out=mx[:, 2 + c:3 + c],
                                       in_=kh[:, c * T:(c + 1) * T], axis=AX.X,
                                       op=ALU.max, apply_absolute_value=True))
            qV.inc(V.drain())
            G.wait_ge(qV.h, qV.n)
            for c in range(2):
                for half in range(2):
                    hl = 2 * c + half
                    qG.inc(G.tensor_reduce(
                        out=mxrow[0:1, hl:hl + 1],
                        in_=mx[half * 64:(half + 1) * 64, c:c + 1],
                        axis=AX.C, op=ALU.max))
                    qG.inc(G.tensor_reduce(
                        out=mxrow[0:1, 4 + hl:5 + hl],
                        in_=mx[half * 64:(half + 1) * 64, 2 + c:3 + c],
                        axis=AX.C, op=ALU.max))
            V.wait_ge(qG.h, qG.n)
            V.wait_ge(qA.h, qA.n)
            qV.inc(V.tensor_tensor(out=nbrow[:], in0=mxrow[0:1, 0:4],
                                   in1=mxrow[0:1, 4:8], op=ALU.mult))
            V.drain()
            qV.inc(V.tensor_scalar(out=nbrow[:], in0=nbrow[:],
                                   scalar1=(-2.0 * ISC),
                                   scalar2=None, op0=ALU.mult))
            qV.inc(V.drain())
            P.wait_ge(qV.h, qV.n)
            BG.wait(P, 4)
            qP.inc(P.matmul(out=pb[4][:, 0:4], lhsT=c_one2[0:1, :], rhs=nbrow[:],
                            start=True, stop=True))
            V.wait_ge(qP.h, qP.n)
            qV.inc(V.tensor_copy(out=negB[:], in_=pb[4][:, 0:4]))
            qV.inc(V.drain())
            BG.set(4, qV)

            if debug and g == 0:
                G.wait_ge(qV.h, qV.n)
                G.wait_ge(qG.h, qG.n)
                dG.inc(G.dma_start(out=dbg_qh, in_=qh[:]), 16)
                dG.inc(G.dma_start(out=dbg_kh, in_=kh[:]), 16)
                dG.inc(G.dma_start(out=dbg_vx, in_=vx[:]), 16)
                dG.inc(G.dma_start(out=dbg_nb, in_=negB[:]), 16)
                dG.inc(G.dma_start(out=dbg_mx, in_=mx[:]), 16)
                dG.inc(G.dma_start(out=dbg_mxr, in_=mxrow[:]), 16)
                dG.inc(G.dma_start(out=dbg_nbr, in_=nbrow[:]), 16)
            P.wait_ge(qG.h, qG.n)
            for hp in range(2):
                c = hp
                for qt in range(8):
                    nkc = 4 * qt + 4
                    for kc in range(nkc):
                        par = kc % 2
                        for h in range(2):
                            bank = 2 * h + par
                            BG.wait(P, bank)
                            mm = P.matmul(
                                out=pb[bank][:],
                                lhsT=kh[h * 64:(h + 1) * 64,
                                        c * T + kc * 128: c * T + (kc + 1) * 128],
                                rhs=qh[h * 64:(h + 1) * 64,
                                       c * T + qt * 512: c * T + (qt + 1) * 512],
                                start=True, stop=True,
                                tile_position=(h * 64, 0))
                        qP.inc(mm)
                        r = kc - 4 * qt
                        diag = r >= 0
                        if diag:
                            V.wait_ge(qP.h, qP.n)
                            for h in range(2):
                                qV.inc(V.tensor_tensor(
                                    out=pb[2 * h + par][:, r * 128:(r + 1) * 128],
                                    in0=pb[2 * h + par][:, r * 128:(r + 1) * 128],
                                    in1=c_tri[:], op=ALU.add))
                            qV.inc(V.drain())
                            if r > 0:
                                G.wait_ge(qP.h, qP.n)
                                for h in range(2):
                                    qG.inc(G.memset(
                                        Pbuf[:, (2 * h + par) * 512:
                                             (2 * h + par) * 512 + r * 128], 0.0))
                        A.wait_ge(qP.h, qP.n)
                        A.wait_ge(qV.h, qV.n)
                        if diag and r > 0:
                            A.wait_ge(qG.h, qG.n)
                        for h in range(2):
                            hl = 2 * hp + h
                            off = r * 128 if (diag and r > 0) else 0
                            qA.inc(A.activation(
                                out=Pbuf[:, (2 * h + par) * 512 + off:
                                         (2 * h + par + 1) * 512],
                                in_=pb[2 * h + par][:, off:512],
                                func=AF.Exp, bias=negB[:, hl:hl + 1], scale=ISC))
                            BG.set(2 * h + par, qA)
                        P.wait_ge(qA.h, qA.n)
                        if kc == 0:
                            BG.wait(P, 4)
                            BG.wait(P, 5)
                        for h in range(2):
                            hl = 2 * hp + h
                            mm = P.matmul(
                                out=pb[4 + h][0:65, :],
                                lhsT=vx[:, kc * 260 + hl * 65:
                                        kc * 260 + (hl + 1) * 65],
                                rhs=Pbuf[:, (2 * h + par) * 512:
                                         (2 * h + par + 1) * 512],
                                start=(kc == 0), stop=(kc == nkc - 1))
                        qP.inc(mm)
                    for h in range(2):
                        hl = 2 * hp + h
                        gh = 4 * g + hl
                        A.wait_ge(qP.h, qP.n)
                        qA.inc(A.activation(out=rd[64:65, :],
                                            in_=pb[4 + h][64:65, :],
                                            func=AF.Ln, bias=c30[64:65, 0:1]))
                        qA.inc(A.activation(out=rd[64:65, :], in_=rd[64:65, :],
                                            func=AF.Exp, scale=-1.0))
                        P.wait_ge(qA.h, qA.n)
                        BG.wait(P, 0)
                        qP.inc(P.matmul(out=pb[0][0:64, :],
                                        lhsT=c_one2[64:65, 0:64],
                                        rhs=rd[64:65, :], start=True, stop=True))
                        V.wait_ge(qP.h, qP.n)
                        V.wait_ge(dG.h, aout_dma[h])
                        qV.inc(V.tensor_copy(out=rb[:], in_=pb[0][0:64, :]))
                        qV.inc(V.tensor_tensor(
                            out=aout_s[:, h * 512:(h + 1) * 512],
                            in0=pb[4 + h][0:64, :], in1=rb[:], op=ALU.mult))
                        BG.set(0, qV)
                        BG.set(4 + h, qV)
                        G.wait_ge(qV.h, qV.n)
                        aout_dma[h] = dG.inc(G.dma_start(
                            out=aout_h[gh * 64:(gh + 1) * 64,
                                       qt * 512:(qt + 1) * 512],
                            in_=aout_s[:, h * 512:(h + 1) * 512]), 16)

        attn_dG = dG.n

        for ec in range(8):
            SY.wait_ge(qV.h, qV.n)
            dS.inc(SY.dma_start(out=wst[:, 0:1024],
                                in_=wT["wo"][ec * 128:(ec + 1) * 128, :]), 16)
            V.wait_ge(dS.h, dS.n)
            V.wait_ge(qP.h, qP.n)
            qV.inc(V.tensor_copy(out=wo_bf[:, ec * 1024:(ec + 1) * 1024],
                                 in_=wst[:, 0:1024]))
        dS.inc(SY.dma_start(out=wst[0:1, 1024:2048], in_=bo_row), 16)
        P.wait_ge(dS.h, dS.n)
        P.wait_ge(qV.h, qV.n)
        BG.wait(P, 3)
        BG.wait(P, 4)
        qP.inc(P.matmul(out=pb[3][:], lhsT=c_one2[0:1, :],
                        rhs=wst[0:1, 1024:1536], start=True, stop=True))
        qP.inc(P.matmul(out=pb[4][:], lhsT=c_one2[0:1, :],
                        rhs=wst[0:1, 1536:2048], start=True, stop=True))
        V.wait_ge(qP.h, qP.n)
        qV.inc(V.tensor_copy(out=bo_b[:, 0:512], in_=pb[3][:]))
        qV.inc(V.tensor_copy(out=bo_b[:, 512:1024], in_=pb[4][:]))
        wo_rdy = qV.n

        SY.wait_ge(dG.h, attn_dG)
        y2_fin = []
        rd_fin = []
        x1s_dma = []
        for tc in range(32):
            par = tc % 2
            if tc >= 2:
                SY.wait_ge(qP.h, rd_fin[tc - 2])
                SY.wait_ge(qV.h, y2_fin[tc - 2])
            for ec in range(8):
                dS.inc(SY.dma_start(
                    out=aord[:, par * 1024 + ec * 128: par * 1024 + (ec + 1) * 128],
                    in_=aout_h[ec * 128:(ec + 1) * 128, tc * 128:(tc + 1) * 128]),
                    16)
            dS.inc(SY.dma_start(out=sc[:, par * 1024:(par + 1) * 1024],
                                in_=x_in[tc * 128:(tc + 1) * 128, :]), 16)
            ld_tc = dS.n
            P.wait_ge(dS.h, ld_tc)
            P.wait_ge(qV.h, wo_rdy if tc == 0 else y2_fin[tc - 1])
            for dh in range(2):
                for ec in range(8):
                    mm = P.matmul(
                        out=pb[dh][:],
                        lhsT=aord[:, par * 1024 + ec * 128:
                                  par * 1024 + (ec + 1) * 128],
                        rhs=wo_bf[:, ec * 1024 + dh * 512:
                                  ec * 1024 + (dh + 1) * 512],
                        start=(ec == 0), stop=(ec == 7))
            rd_fin.append(qP.inc(mm))
            V.wait_ge(qP.h, qP.n)
            for dh in range(2):
                qV.inc(V.tensor_tensor(
                    out=zt[:, dh * 512:(dh + 1) * 512], in0=pb[dh][:],
                    in1=sc[:, par * 1024 + dh * 512: par * 1024 + (dh + 1) * 512],
                    op=ALU.add))
            y2_fin.append(qV.n)
            qV.inc(V.tensor_tensor(out=zt[:], in0=zt[:], in1=bo_b[:], op=ALU.add))
            A.wait_ge(qV.h, qV.n)
            qA.inc(A.activation(out=Pbuf[:, 0:1024], in_=zt[:], func=AF.Square,
                                accum_out=lnst[:, 1:2]))
            qA.inc(A.drain())
            V.wait_ge(qA.h, qA.n)
            qV.inc(V.reduce_sum(out=lnst[:, 0:1], in_=zt[:], axis=AX.X))
            V.drain()
            qV.inc(V.tensor_scalar(out=lnst[:, 2:3], in0=lnst[:, 0:1],
                                   scalar1=1.0 / D, scalar2=None, op0=ALU.mult))
            V.drain()
            qV.inc(V.tensor_tensor(out=lnst[:, 3:4], in0=lnst[:, 2:3],
                                   in1=lnst[:, 2:3], op=ALU.mult))
            V.drain()
            qV.inc(V.scalar_tensor_tensor(out=lnst[:, 3:4], in0=lnst[:, 1:2],
                                          scalar=1.0 / D, in1=lnst[:, 3:4],
                                          op0=ALU.mult, op1=ALU.subtract))
            qV.inc(V.drain())
            A.wait_ge(qV.h, qV.n)
            qA.inc(A.activation(out=lnst[:, 3:4], in_=lnst[:, 3:4], func=AF.Ln,
                                bias=cEPS[:, 0:1]))
            A.drain()
            qA.inc(A.activation(out=lnst[:, 4:5], in_=lnst[:, 3:4], func=AF.Exp,
                                scale=-0.5))
            qA.inc(A.drain())
            V.wait_ge(qA.h, qA.n)
            if len(x1s_dma) >= 2:
                V.wait_ge(dG.h, x1s_dma[-2])
            qV.inc(V.tensor_scalar(out=x1f[:], in0=zt[:], scalar1=lnst[:, 2:3],
                                   scalar2=lnst[:, 4:5], op0=ALU.subtract,
                                   op1=ALU.mult))
            qV.inc(V.tensor_scalar(out=x1b[:], in0=zt[:], scalar1=lnst[:, 2:3],
                                   scalar2=lnst[:, 4:5], op0=ALU.subtract,
                                   op1=ALU.mult))
            x1_rdy = qV.n
            G.wait_ge(qV.h, x1_rdy)
            dG.inc(G.dma_start(out=x1_out[tc * 128:(tc + 1) * 128, :], in_=x1f[:]),
                   16)
            ev = []
            for j in range(8):
                P.wait_ge(qV.h, x1_rdy if j < 2 else ev[j - 2])
                qP.inc(P.transpose(out=tps[j % 2][:, 0:128],
                                   in_=x1b[:, j * 128:(j + 1) * 128],
                                   identity=c_idn[:]))
                V.wait_ge(qP.h, qP.n)
                ev.append(qV.inc(V.tensor_copy(
                    out=x1fmt[:, j * 128:(j + 1) * 128],
                    in_=tps[j % 2][:, 0:128])))
            qV.inc(V.drain())
            G.wait_ge(qV.h, qV.n)
            for j in range(8):
                dG.inc(G.dma_start(
                    out=x1fmT_out[j * 128:(j + 1) * 128, tc * 128:(tc + 1) * 128],
                    in_=x1fmt[:, j * 128:(j + 1) * 128]), 16)
            x1s_dma.append(dG.n)
            P.wait_ge(qV.h, qV.n)
            P.wait_ge(qA.h, qA.n)
            BG.wait(P, 2)
            for ec in range(8):
                mm = P.matmul(out=pb[2][:, 0:8],
                              lhsT=x1fmt[:, ec * 128:(ec + 1) * 128],
                              rhs=rw_bf[:, ec * 8:(ec + 1) * 8],
                              start=(ec == 0), stop=(ec == 7))
            qP.inc(mm)
            V.wait_ge(qP.h, qP.n)
            qV.inc(V.reduce_max(out=lnst[:, 5:6], in_=pb[2][:, 0:8], axis=AX.X))
            V.drain()
            qV.inc(V.tensor_scalar(out=lnst[:, 6:7], in0=lnst[:, 5:6],
                                   scalar1=-1.0, scalar2=None, op0=ALU.mult))
            qV.inc(V.drain())
            A.wait_ge(qV.h, qV.n)
            qA.inc(A.activation(out=e8[:], in_=pb[2][:, 0:8], func=AF.Exp,
                                bias=lnst[:, 6:7], accum_out=lnst[:, 7:8]))
            qA.inc(A.drain())
            V.wait_ge(qA.h, qA.n)
            qV.inc(V.reciprocal(out=route_p[:, tc:tc + 1], in_=lnst[:, 7:8]))
            qV.inc(V.tensor_scalar(out=oh8[:], in0=pb[2][:, 0:8],
                                   scalar1=lnst[:, 5:6], scalar2=None,
                                   op0=ALU.is_ge))
            V.drain()
            qV.inc(V.scalar_tensor_tensor(out=j8[:], in0=oh8[:], scalar=1.0,
                                          in1=c_io8[:], op0=ALU.mult,
                                          op1=ALU.mult,
                                          accum_out=route_i[:, tc:tc + 1]))
            BG.set(2, qV)
        qV.inc(V.drain())
        G.wait_ge(qV.h, qV.n)
        dG.inc(G.dma_start(
            out=route_out[0:1, :].rearrange("o (p c) -> (o p) c", p=128),
            in_=route_p[:]), 16)
        dG.inc(G.dma_start(
            out=route_out[1:2, :].rearrange("o (p c) -> (o p) c", p=128),
            in_=route_i[:]), 16)
        G.wait_ge(dG.h, dG.n)
        G.wait_ge(dS.h, dS.n)
    es.close()
    return nc


def build_l2():
    nc = bass.Bass("TRN2", target_bir_lowering=False, debug=False)
    dr = lambda n, s, d, k="ExternalInput": nc.dram_tensor(n, s, d, kind=k).ap()
    x1_d = dr("x1", [T, D], F32)
    x1fmT_d = dr("x1fmT", [D, T], BF16)
    myr_d = dr("my_route", [2, T], F32)
    allr_d = dr("routes_all", [2 * NE, T], F32)
    w1_d = dr("W1", [NE, D, DFF], F32)
    w2_d = dr("W2", [NE, DFF, D], F32)
    b1c_d = dr("b1cols", [NE, 128, 32], F32)
    b2r_d = dr("b2row", [1, NE * D], F32)
    onesb_d = dr("ones_b", [1, 128], BF16)
    onesc_d = dr("ones_c", [128, 1], F32)
    onef_d = dr("ones_f", [1, 128], F32)
    out_d = dr("out_g", [T, D], F32, "ExternalOutput")
    acc_h = nc.dram_tensor("acc_h", [T, D], BF16).ap()

    es = ExitStack()
    sb = lambda n, s, d: es.enter_context(nc.sbuf_tensor(n, s, d))
    w1bf = sb("w1bf", [128, 8 * DFF], BF16)
    w2bf = sb("w2bf", [128, 32 * D], BF16)
    wstg = sb("wstg", [128, 4096], F32)
    xblk = sb("xblk", [128, 2 * 8 * 384], BF16)
    hbf = sb("hbf", [128, 2 * 384], BF16)
    acci = sb("acci", [128, 2 * 3 * 1024], BF16)
    acco = sb("acco", [128, 2 * 3 * 1024], BF16)
    pall = sb("pall", [128, 256], F32)
    iall = sb("iall", [128, 256], F32)
    ploc = sb("ploc", [128, 32], F32)
    iloc = sb("iloc", [128, 32], F32)
    pm = sb("pm", [128, 2048], F32)
    ge = sb("ge", [128, 256], F32)
    ohb = sb("ohb", [128, 256], F32)
    cntp = sb("cntp", [128, 8], F32)
    masks = sb("masks", [128, 256], F32)
    kbuf = sb("kbuf", [128, 32], F32)
    lo = sb("lo", [1, 8], F32)
    hi = sb("hi", [1, 8], F32)
    midr = sb("midr", [1, 8], F32)
    cond = sb("cond", [1, 8], F32)
    mid_b = sb("mid_b", [128, 8], F32)
    tr8 = sb("tr8", [1, 8], F32)
    lo_b = sb("lo_b", [128, 8], F32)
    b1c = sb("b1c", [128, 32], F32)
    b2e = sb("b2e", [1, D], F32)
    b2be = sb("b2be", [1, D], BF16)
    zt = sb("zt", [128, D], F32)
    lnst = sb("lnst", [128, 8], F32)
    c_onesb = sb("c_onesb", [1, 128], BF16)
    c_onesc = sb("c_onesc", [128, 1], F32)
    c_onef = sb("c_onef", [1, 128], F32)
    cEPS = sb("cEPS", [128, 1], F32)

    pb = [es.enter_context(nc.psum_tensor("pb%d" % i, [128, 512], F32))
          for i in range(8)]

    dS = S(nc, es, "dS")
    dG = S(nc, es, "dG")
    qP = S(nc, es, "qP")
    qV = S(nc, es, "qV")
    qA = S(nc, es, "qA")

    P, V, A, G, SY = nc.tensor, nc.vector, nc.scalar, nc.gpsimd, nc.sync
    blk_off = [0]
    for b_ in BLKS:
        blk_off.append(blk_off[-1] + b_)

    with nc.Block():
        qV.inc(V.memset(cEPS[:], EPS))
        for dst, src in [(c_onesb, onesb_d), (c_onesc, onesc_d),
                         (c_onef, onef_d)]:
            dS.inc(SY.dma_start(out=dst[:], in_=src), 16)
        dS.inc(SY.dma_start(out=b2e[:], in_=b2r_d[0:1, 0:D]), 16)
        for a in range(8):
            dS.inc(SY.dma_start(
                out=pall[a * 16:(a + 1) * 16, :],
                in_=allr_d[2 * a:2 * a + 1, :]
                    .rearrange("o (b c) -> (o b) c", b=16)), 16)
            dS.inc(SY.dma_start(
                out=iall[a * 16:(a + 1) * 16, :],
                in_=allr_d[2 * a + 1:2 * a + 2, :]
                    .rearrange("o (b c) -> (o b) c", b=16)), 16)
        dS.inc(SY.dma_start(
            out=ploc[:], in_=myr_d[0:1, :].rearrange("o (p c) -> (o p) c", p=128)),
            16)
        dS.inc(SY.dma_start(
            out=iloc[:], in_=myr_d[1:2, :].rearrange("o (p c) -> (o p) c", p=128)),
            16)
        V.wait_ge(dS.h, dS.n)
        qV.inc(V.tensor_copy(out=b2be[:], in_=b2e[:]))
        b2_rdy = qV.n
        for e in range(NE):
            qV.inc(V.tensor_scalar(out=ohb[:], in0=iall[:], scalar1=float(e),
                                   scalar2=None, op0=ALU.is_equal))
            V.drain()
            qV.inc(V.tensor_tensor(out=pm[:, e * 256:(e + 1) * 256], in0=pall[:],
                                   in1=ohb[:], op=ALU.mult))
        qV.inc(V.memset(lo[:], 0.0))
        qV.inc(V.memset(hi[:], 1.0))
        qV.inc(V.drain())
        for it in range(NITER):
            qV.inc(V.tensor_tensor(out=midr[:], in0=lo[:], in1=hi[:], op=ALU.add))
            V.drain()
            qV.inc(V.tensor_scalar(out=midr[:], in0=midr[:], scalar1=0.5,
                                   scalar2=None, op0=ALU.mult))
            qV.inc(V.drain())
            P.wait_ge(qV.h, qV.n)
            qP.inc(P.matmul(out=pb[0][:, 0:8], lhsT=c_onef[:], rhs=midr[:],
                            start=True, stop=True))
            V.wait_ge(qP.h, qP.n)
            qV.inc(V.tensor_copy(out=mid_b[:], in_=pb[0][:, 0:8]))
            V.drain()
            for e in range(NE):
                qV.inc(V.tensor_scalar(out=ge[:], in0=pm[:, e * 256:(e + 1) * 256],
                                       scalar1=mid_b[:, e:e + 1], scalar2=None,
                                       op0=ALU.is_ge, op1=ALU.add,
                                       accum_out=cntp[:, e:e + 1]))
            qV.inc(V.drain())
            P.wait_ge(qV.h, qV.n)
            qP.inc(P.matmul(out=pb[1][0:1, 0:8], lhsT=c_onesc[:], rhs=cntp[:],
                            start=True, stop=True))
            V.wait_ge(qP.h, qP.n)
            qV.inc(V.tensor_scalar(out=cond[:], in0=pb[1][0:1, 0:8], scalar1=CAP,
                                   scalar2=None, op0=ALU.is_ge))
            V.drain()
            qV.inc(V.tensor_tensor(out=tr8[:], in0=midr[:], in1=lo[:],
                                   op=ALU.subtract))
            V.drain()
            qV.inc(V.tensor_tensor(out=tr8[:], in0=tr8[:], in1=cond[:],
                                   op=ALU.mult))
            V.drain()
            qV.inc(V.tensor_tensor(out=lo[:], in0=lo[:], in1=tr8[:], op=ALU.add))
            V.drain()
            qV.inc(V.tensor_tensor(out=tr8[:], in0=hi[:], in1=midr[:],
                                   op=ALU.subtract))
            V.drain()
            qV.inc(V.tensor_tensor(out=tr8[:], in0=tr8[:], in1=cond[:],
                                   op=ALU.mult))
            V.drain()
            qV.inc(V.tensor_tensor(out=hi[:], in0=midr[:], in1=tr8[:], op=ALU.add))
            V.drain()
        P.wait_ge(qV.h, qV.n)
        qP.inc(P.matmul(out=pb[0][:, 0:8], lhsT=c_onef[:], rhs=lo[:],
                        start=True, stop=True))
        V.wait_ge(qP.h, qP.n)
        qV.inc(V.tensor_copy(out=lo_b[:], in_=pb[0][:, 0:8]))
        V.drain()
        for e in range(NE):
            qV.inc(V.tensor_scalar(out=masks[:, e * 32:(e + 1) * 32], in0=ploc[:],
                                   scalar1=lo_b[:, e:e + 1], scalar2=None,
                                   op0=ALU.is_ge))
        for e in range(NE):
            qV.inc(V.tensor_scalar(out=ohb[:, e * 32:(e + 1) * 32], in0=iloc[:],
                                   scalar1=float(e), scalar2=None,
                                   op0=ALU.is_equal))
        qV.inc(V.drain())
        for e in range(NE):
            qV.inc(V.tensor_tensor(out=masks[:, e * 32:(e + 1) * 32],
                                   in0=masks[:, e * 32:(e + 1) * 32],
                                   in1=ohb[:, e * 32:(e + 1) * 32], op=ALU.mult))
        qV.inc(V.drain())
        qV.inc(V.tensor_copy(out=kbuf[:], in_=masks[:, 0:32]))
        for e in range(1, NE):
            V.drain()
            qV.inc(V.tensor_tensor(out=kbuf[:], in0=kbuf[:],
                                   in1=masks[:, e * 32:(e + 1) * 32],
                                   op=ALU.add))
        V.drain()
        qV.inc(V.tensor_scalar(out=kbuf[:], in0=kbuf[:], scalar1=-1.0,
                               scalar2=1.0, op0=ALU.mult, op1=ALU.add))
        V.drain()
        qV.inc(V.tensor_tensor(out=masks[:, 0:32], in0=masks[:, 0:32],
                               in1=kbuf[:], op=ALU.add))
        qV.inc(V.drain())
        masks_rdy = qV.n

        def load_w1(e, free_qp):
            for ec in range(8):
                for hh in range(2):
                    pi = hh
                    SY.wait_ge(qV.h, qV.n)
                    dS.inc(SY.dma_start(
                        out=wstg[:, pi * 2048:(pi + 1) * 2048],
                        in_=w1_d[e, ec * 128:(ec + 1) * 128,
                                 hh * 2048:(hh + 1) * 2048]), 16)
                    V.wait_ge(dS.h, dS.n)
                    V.wait_ge(qP.h, free_qp)
                    qV.inc(V.tensor_copy(
                        out=w1bf[:, ec * DFF + hh * 2048:
                                 ec * DFF + (hh + 1) * 2048],
                        in_=wstg[:, pi * 2048:(pi + 1) * 2048]))
            return qV.n

        def load_w2(e, free_qp):
            for pc in range(8):
                for hh in range(2):
                    pi = hh
                    SY.wait_ge(qV.h, qV.n)
                    dS.inc(SY.dma_start(
                        out=wstg[:, pi * 2048:(pi + 1) * 2048]
                            .rearrange("p (c q) -> p c q", q=1024),
                        in_=w2_d[e, pc * 512:(pc + 1) * 512, :]
                            .rearrange("(c p) q -> p c q", p=128)
                            [:, hh * 2:(hh + 1) * 2, :]), 16)
                    V.wait_ge(dS.h, dS.n)
                    V.wait_ge(qP.h, free_qp)
                    qV.inc(V.tensor_copy(
                        out=w2bf[:, pc * 4096 + hh * 2048:
                                 pc * 4096 + (hh + 1) * 2048],
                        in_=wstg[:, pi * 2048:(pi + 1) * 2048]))
            return qV.n

        w1_rdy = load_w1(0, 0)
        w2_rdy = load_w2(0, 0)
        dS.inc(SY.dma_start(out=b1c[:], in_=b1c_d[0]), 16)
        b1_rdy = dS.n
        h_last_of_e = 0
        stt_done = 0
        acc_wr = {}
        relu_ct = []
        for e in range(NE):
            for bi, BL in enumerate(BLKS):
                idx = e * len(BLKS) + bi
                c0 = blk_off[bi]
                par = bi % 2
                ntc = BL // 128
                if idx >= 2:
                    G.wait_ge(qP.h, qP.n)
                for ec in range(8):
                    dG.inc(G.dma_start(
                        out=xblk[:, par * 3072 + ec * 384:
                                 par * 3072 + ec * 384 + BL],
                        in_=x1fmT_d[ec * 128:(ec + 1) * 128, c0:c0 + BL]), 16)
                xb_rdy = dG.n
                if e > 0:
                    SY.wait_ge(dG.h, acc_wr[(e - 1, bi)])
                    SY.wait_ge(qV.h, stt_done)
                    dS.inc(SY.dma_start(
                        out=acci[:, par * 3072: par * 3072 + ntc * 1024]
                            .rearrange("p (c q) -> p c q", q=1024),
                        in_=acc_h[c0:c0 + BL, :]
                            .rearrange("(c p) q -> p c q", p=128)), 16)
                acc_rdy = dS.n
                relu_ct = []
                for dffc in range(32):
                    hb = dffc % 2
                    if dffc == 0:
                        P.wait_ge(qV.h, w1_rdy)
                        P.wait_ge(dG.h, xb_rdy)
                        P.wait_ge(qV.h, stt_done)
                        P.wait_ge(dS.h, b1_rdy)
                        P.wait_ge(qA.h, qA.n)
                    if dffc >= 2:
                        P.wait_ge(qA.h, relu_ct[dffc - 2])
                    for ec in range(8):
                        mm = P.matmul(
                            out=pb[hb][:, 0:BL],
                            lhsT=w1bf[:, ec * DFF + dffc * 128:
                                      ec * DFF + (dffc + 1) * 128],
                            rhs=xblk[:, par * 3072 + ec * 384:
                                     par * 3072 + ec * 384 + BL],
                            start=(ec == 0), stop=(ec == 7))
                    qP.inc(mm)
                    A.wait_ge(qP.h, qP.n)
                    qA.inc(A.activation(
                        out=hbf[:, hb * 384: hb * 384 + BL],
                        in_=pb[hb][:, 0:BL], func=AF.Relu,
                        bias=b1c[:, dffc:dffc + 1]))
                    relu_ct.append(qA.n)
                    P.wait_ge(qA.h, qA.n)
                    if dffc == 0:
                        P.wait_ge(qV.h, w2_rdy)
                    for tc3 in range(ntc):
                        for dh in range(2):
                            mm = P.matmul(
                                out=pb[2 + tc3 * 2 + dh][:],
                                lhsT=hbf[:, hb * 384 + tc3 * 128:
                                         hb * 384 + (tc3 + 1) * 128],
                                rhs=w2bf[:, dffc * D + dh * 512:
                                         dffc * D + (dh + 1) * 512],
                                start=(dffc == 0), stop=False)
                    qP.inc(mm)
                P.wait_ge(qV.h, b2_rdy)
                for tc3 in range(ntc):
                    for dh in range(2):
                        mm = P.matmul(
                            out=pb[2 + tc3 * 2 + dh][:], lhsT=c_onesb[:],
                            rhs=b2be[0:1, dh * 512:(dh + 1) * 512],
                            start=False, stop=True)
                qP.inc(mm)
                if e == 0 and bi == 0:
                    V.wait_ge(qV.h, masks_rdy)
                V.wait_ge(qP.h, qP.n)
                if e > 0:
                    V.wait_ge(dS.h, acc_rdy)
                if idx >= 2:
                    V.wait_ge(dG.h, acc_wr.get((e, bi - 2),
                                               acc_wr.get((e - 1, bi + len(BLKS) - 2), 0)))
                for tc3 in range(ntc):
                    tcg = c0 // 128 + tc3
                    for dh in range(2):
                        dst = acco[:, par * 3072 + tc3 * 1024 + dh * 512:
                                   par * 3072 + tc3 * 1024 + (dh + 1) * 512]
                        mcol = masks[:, e * 32 + tcg: e * 32 + tcg + 1]
                        if e == 0:
                            qV.inc(V.tensor_scalar(
                                out=dst, in0=pb[2 + tc3 * 2 + dh][:],
                                scalar1=mcol, scalar2=None, op0=ALU.mult))
                        else:
                            qV.inc(V.scalar_tensor_tensor(
                                out=dst, in0=pb[2 + tc3 * 2 + dh][:],
                                scalar=mcol,
                                in1=acci[:, par * 3072 + tc3 * 1024 + dh * 512:
                                         par * 3072 + tc3 * 1024 + (dh + 1) * 512],
                                op0=ALU.mult, op1=ALU.add))
                stt_done = qV.n
                G.wait_ge(qV.h, stt_done)
                acc_wr[(e, bi)] = dG.inc(G.dma_start(
                    out=acc_h[c0:c0 + BL, :].rearrange("(c p) q -> p c q", p=128),
                    in_=acco[:, par * 3072: par * 3072 + ntc * 1024]
                        .rearrange("p (c q) -> p c q", q=1024)), 16)
                if bi == len(BLKS) - 1:
                    h_last_of_e = qP.n
            if e + 1 < NE:
                w1_rdy = load_w1(e + 1, h_last_of_e)
                w2_rdy = load_w2(e + 1, qP.n)
                SY.wait_ge(qA.h, qA.n)
                dS.inc(SY.dma_start(out=b1c[:], in_=b1c_d[e + 1]), 16)
                b1_rdy = dS.n
                SY.wait_ge(qP.h, qP.n)
                dS.inc(SY.dma_start(out=b2e[:],
                                    in_=b2r_d[0:1, (e + 1) * D:(e + 2) * D]), 16)
                V.wait_ge(dS.h, dS.n)
                qV.inc(V.tensor_copy(out=b2be[:], in_=b2e[:]))
                b2_rdy = qV.n

        all_acc = dG.n
        ostore = []
        for tc in range(32):
            par = tc % 2
            if tc == 0:
                SY.wait_ge(dG.h, all_acc)
            if tc >= 2:
                SY.wait_ge(qV.h, qV.n)
            dS.inc(SY.dma_start(out=wstg[:, par * D:(par + 1) * D],
                                in_=x1_d[tc * 128:(tc + 1) * 128, :]), 16)
            dS.inc(SY.dma_start(out=acci[:, par * D:(par + 1) * D],
                                in_=acc_h[tc * 128:(tc + 1) * 128, :]), 16)
            V.wait_ge(dS.h, dS.n)
            qV.inc(V.tensor_tensor(out=zt[:], in0=wstg[:, par * D:(par + 1) * D],
                                   in1=acci[:, par * D:(par + 1) * D],
                                   op=ALU.add))
            A.wait_ge(qV.h, qV.n)
            qA.inc(A.activation(out=xblk[:, 0:1024], in_=zt[:], func=AF.Square,
                                accum_out=lnst[:, 1:2]))
            qA.inc(A.drain())
            V.wait_ge(qA.h, qA.n)
            qV.inc(V.reduce_sum(out=lnst[:, 0:1], in_=zt[:], axis=AX.X))
            V.drain()
            qV.inc(V.tensor_scalar(out=lnst[:, 2:3], in0=lnst[:, 0:1],
                                   scalar1=1.0 / D, scalar2=None, op0=ALU.mult))
            V.drain()
            qV.inc(V.tensor_tensor(out=lnst[:, 3:4], in0=lnst[:, 2:3],
                                   in1=lnst[:, 2:3], op=ALU.mult))
            V.drain()
            qV.inc(V.scalar_tensor_tensor(out=lnst[:, 3:4], in0=lnst[:, 1:2],
                                          scalar=1.0 / D, in1=lnst[:, 3:4],
                                          op0=ALU.mult, op1=ALU.subtract))
            qV.inc(V.drain())
            A.wait_ge(qV.h, qV.n)
            qA.inc(A.activation(out=lnst[:, 3:4], in_=lnst[:, 3:4], func=AF.Ln,
                                bias=cEPS[:, 0:1]))
            A.drain()
            qA.inc(A.activation(out=lnst[:, 4:5], in_=lnst[:, 3:4], func=AF.Exp,
                                scale=-0.5))
            qA.inc(A.drain())
            V.wait_ge(qA.h, qA.n)
            if len(ostore) >= 2:
                V.wait_ge(dG.h, ostore[-2])
            qV.inc(V.tensor_scalar(out=pm[:, par * D:(par + 1) * D], in0=zt[:],
                                   scalar1=lnst[:, 2:3], scalar2=lnst[:, 4:5],
                                   op0=ALU.subtract, op1=ALU.mult))
            G.wait_ge(qV.h, qV.n)
            ostore.append(dG.inc(G.dma_start(
                out=out_d[tc * 128:(tc + 1) * 128, :],
                in_=pm[:, par * D:(par + 1) * D]), 16))
        G.wait_ge(dG.h, dG.n)
        G.wait_ge(dS.h, dS.n)
    es.close()
    return nc


_NC_CACHE = {}


def _get_nc(which):
    if which not in _NC_CACHE:
        _NC_CACHE[which] = build_l1() if which == "l1" else build_l2()
    return _NC_CACHE[which]


def kernel(L0_attn_x, L0_attn_Wk, L0_attn_Wq, L0_attn_Wv, L0_attn_Wo,
           L0_attn_bo, L0_attn_kln_g, L0_attn_kln_b, L0_attn_qln_g,
           L0_attn_qln_b, L0_ffn_router_w, L0_ffn_W1, L0_ffn_b1, L0_ffn_W2,
           L0_ffn_b2):
    x = np.ascontiguousarray(np.asarray(L0_attn_x, dtype=np.float32))
    cst = host_consts()
    C = np.ascontiguousarray

    shared1 = {
        "wqT": C(np.asarray(L0_attn_Wq, np.float32).T),
        "wkT": C(np.asarray(L0_attn_Wk, np.float32).T),
        "wvT": C(np.asarray(L0_attn_Wv, np.float32).T),
        "woT": C(np.asarray(L0_attn_Wo, np.float32).T),
        "rwT": C(np.asarray(L0_ffn_router_w, np.float32).T),
        "bo_row": np.asarray(L0_attn_bo, np.float32).reshape(1, D),
        "qg_c": C(np.tile(np.asarray(L0_attn_qln_g, np.float32), 2)).reshape(128, 1),
        "qb_c": C(np.tile(np.asarray(L0_attn_qln_b, np.float32), 2)).reshape(128, 1),
        "kg_c": C(np.tile(np.asarray(L0_attn_kln_g, np.float32), 2)).reshape(128, 1),
        "kb_c": C(np.tile(np.asarray(L0_attn_kln_b, np.float32), 2)).reshape(128, 1),
        "idn_bf": cst["idn_bf"], "ones2": cst["ones2"], "hs_bf": cst["hs_bf"],
        "sel_bf": cst["sel_bf"], "tri": cst["tri"], "iota8": cst["iota8"],
    }
    in_maps1 = []
    for c in range(NCORE):
        m = dict(shared1)
        m["x"] = x[c]
        m["xT"] = C(x[c].T)
        in_maps1.append(m)

    nc1 = _get_nc("l1")
    r1 = run_bass_kernel_spmd(nc1, in_maps1, core_ids=list(range(NCORE)))

    routes_all = C(np.concatenate(
        [np.asarray(r1.results[c]["route_out"]) for c in range(NCORE)],
        axis=0).astype(np.float32))

    b1cols = C(np.asarray(L0_ffn_b1, np.float32).reshape(NE, 32, 128)
               .transpose(0, 2, 1))
    shared2 = {
        "routes_all": routes_all,
        "W1": np.asarray(L0_ffn_W1, np.float32),
        "W2": np.asarray(L0_ffn_W2, np.float32),
        "b1cols": b1cols,
        "b2row": np.asarray(L0_ffn_b2, np.float32).reshape(1, NE * D),
        "ones_b": cst["ones_b"], "ones_c": cst["ones_c"],
        "ones_f": cst["ones_f"],
    }
    in_maps2 = []
    for c in range(NCORE):
        m = dict(shared2)
        m["x1"] = np.asarray(r1.results[c]["x1_out"])
        m["x1fmT"] = np.asarray(r1.results[c]["x1fmT_out"])
        m["my_route"] = np.asarray(r1.results[c]["route_out"])
        in_maps2.append(m)

    nc2 = _get_nc("l2")
    r2 = run_bass_kernel_spmd(nc2, in_maps2, core_ids=list(range(NCORE)))

    out = np.stack([np.asarray(r2.results[c]["out_g"]) for c in range(NCORE)],
                   axis=0)
    return out.astype(np.float32)


if __name__ == "__main__":
    print("building l1...")
    build_l1()
    print("l1 ok; building l2...")
    build_l2()
    print("l2 ok")


# version 05
# speedup vs baseline: 1.4030x; 1.4030x over previous
import numpy as np
import ml_dtypes
import concourse.bass as bass
import concourse.mybir as mybir
from concourse.bass_utils import run_bass_kernel_spmd
from contextlib import ExitStack

F32 = mybir.dt.float32
BF16 = mybir.dt.bfloat16
AF = mybir.ActivationFunctionType
ALU = mybir.AluOpType
AX = mybir.AxisListType

B, T, D = 8, 4096, 1024
DFF, NE = 4096, 8
CAP = 5120.0
NEGM = -1.0e30
EPS = 1e-5
ISC = 1.0 / 32.0
NITER = 30
BLKS = [384] * 10 + [256]
NCORE = 8


class S(object):
    def __init__(self, nc, es, name):
        self.h = es.enter_context(nc.semaphore(name))
        self.n = 0

    def inc(self, inst, k=1):
        inst.then_inc(self.h, k)
        self.n += k
        return self.n


class BankGate(object):

    def __init__(self):
        self.need = {}

    def wait(self, eng, bank):
        for sem, cnt in self.need.get(bank, []):
            eng.wait_ge(sem.h, cnt)

    def set(self, bank, *pairs):
        self.need[bank] = [(s, s.n) for s in pairs]


def _bf(a):
    return np.ascontiguousarray(np.asarray(a)).astype(ml_dtypes.bfloat16)


def host_consts():
    c = {}
    c["idn_bf"] = _bf(np.eye(128, dtype=np.float32))
    ones128 = np.ones((128, 128), np.float32)
    c["ones2"] = ones128
    c["ones_f"] = np.ones((1, 128), np.float32)
    c["ones_b"] = _bf(np.ones((1, 128), np.float32))
    c["ones_c"] = np.ones((128, 1), np.float32)
    hs = np.zeros((128, 32), np.float32)
    for p in range(128):
        for tt in range(4):
            hs[p, tt * 8 + 2 * tt + p // 64] = 1.0
    c["hs_bf"] = _bf(hs)
    sel = np.zeros((8, 512), np.float32)
    for p in range(128):
        for tt in range(4):
            sel[2 * tt + p // 64, tt * 128 + p] = 1.0
    c["sel_bf"] = _bf(sel)
    tri = np.zeros((128, 128), np.float32)
    for j in range(128):
        tri[j, :j] = NEGM
    c["tri"] = tri
    c["iota8"] = np.ascontiguousarray(
        np.broadcast_to(np.arange(8, dtype=np.float32), (128, 8)))
    return c


def build_l1(debug=False):
    nc = bass.Bass("TRN2", target_bir_lowering=False, debug=False)
    dr = lambda n, s, d, k="ExternalInput": nc.dram_tensor(n, s, d, kind=k).ap()
    xT = dr("xT", [D, T], F32)
    x_in = dr("x", [T, D], F32)
    wT = {w: dr(w + "T", [D, D], F32) for w in ("wq", "wk", "wv", "wo")}
    rwT = dr("rwT", [D, NE], F32)
    bo_row = dr("bo_row", [1, D], F32)
    cols_d = {n: dr(n, [128, 1], F32) for n in ("qg_c", "qb_c", "kg_c", "kb_c")}
    idn_d = dr("idn_bf", [128, 128], BF16)
    ones2_d = dr("ones2", [128, 128], F32)
    hs_d = dr("hs_bf", [128, 32], BF16)
    sel_d = dr("sel_bf", [8, 512], BF16)
    tri_d = dr("tri", [128, 128], F32)
    io8_d = dr("iota8", [128, 8], F32)
    x1_out = dr("x1_out", [T, D], F32, "ExternalOutput")
    x1fmT_out = dr("x1fmT_out", [D, T], BF16, "ExternalOutput")
    route_out = dr("route_out", [2, T], F32, "ExternalOutput")
    aout_h = nc.dram_tensor("aout_h", [D, T], BF16,
                            kind="ExternalOutput" if debug else "Internal").ap()
    if debug:
        dbg_qh = dr("dbg_qh", [128, 2 * T], BF16, "ExternalOutput")
        dbg_kh = dr("dbg_kh", [128, 2 * T], BF16, "ExternalOutput")
        dbg_vx = dr("dbg_vx", [128, 32 * 4 * 65], BF16, "ExternalOutput")
        dbg_nb = dr("dbg_nb", [128, 4], F32, "ExternalOutput")
        dbg_mx = dr("dbg_mx", [128, 4], F32, "ExternalOutput")
        dbg_mxr = dr("dbg_mxr", [1, 8], F32, "ExternalOutput")
        dbg_nbr = dr("dbg_nbr", [1, 4], F32, "ExternalOutput")

    es = ExitStack()
    sb = lambda n, s, d: es.enter_context(nc.sbuf_tensor(n, s, d))
    xfm = sb("xfm", [128, 8 * T], BF16)
    qh = sb("qh", [128, 2 * T], BF16)
    kh = sb("kh", [128, 2 * T], BF16)
    vx = sb("vx", [128, 32 * 4 * 65], BF16)
    wst = sb("wst", [128, 2048], F32)
    wbf = sb("wbf", [128, 2048], BF16)
    qf32 = sb("qf32", [128, 2048], F32)
    qbf_s = sb("qbf_s", [128, 512], BF16)
    q2bf_s = sb("q2bf_s", [128, 512], BF16)
    stA = sb("stA", [8, 1024], F32)
    stR = sb("stR", [8, 512], BF16)
    stMR = sb("stMR", [8, 512], BF16)
    mx = sb("mx", [128, 4], F32)
    mxrow = sb("mxrow", [1, 8], F32)
    nbrow = sb("nbrow", [1, 4], F32)
    negB = sb("negB", [128, 4], F32)
    ap_s = sb("ap_s", [128, 512], F32)
    Pbuf = sb("Pbuf", [128, 2048], BF16)
    rd = sb("rd", [128, 512], F32)
    rb = sb("rb", [64, 512], F32)
    aout_s = sb("aout_s", [64, 1024], BF16)
    aord = sb("aord", [128, 2048], BF16)
    wo_bf = sb("wo_bf", [128, 8192], BF16)
    bo_b = sb("bo_b", [128, D], F32)
    sc = sb("sc", [128, 2048], F32)
    zt = sb("zt", [128, D], F32)
    x1f = sb("x1f", [128, D], F32)
    x1b = sb("x1b", [128, D], BF16)
    x1fmt = sb("x1fmt", [128, D], BF16)
    e8 = sb("e8", [128, 8], F32)
    oh8 = sb("oh8", [128, 8], F32)
    j8 = sb("j8", [128, 8], F32)
    lnst = sb("lnst", [128, 8], F32)
    route_p = sb("route_p", [128, 32], F32)
    route_i = sb("route_i", [128, 32], F32)
    rwst = sb("rwst", [128, 64], F32)
    rw_bf = sb("rw_bf", [128, 64], BF16)
    c_idn = sb("c_idn", [128, 128], BF16)
    c_one2 = sb("c_one2", [128, 128], F32)
    c_hs = sb("c_hs", [128, 32], BF16)
    c_sel = sb("c_sel", [8, 512], BF16)
    c_tri = sb("c_tri", [128, 128], F32)
    c_io8 = sb("c_io8", [128, 8], F32)
    c_col = {n: sb("c_" + n, [128, 1], F32) for n in ("qg_c", "qb_c", "kg_c", "kb_c")}
    cEPS = sb("cEPS", [128, 1], F32)
    c30 = sb("c30", [128, 1], F32)

    sAB = [es.enter_context(nc.psum_tensor("sAB%d" % i, [128, 1024], F32))
           for i in range(2)]
    pO_ = es.enter_context(nc.psum_tensor("pO", [128, 1024], F32))
    tps = [es.enter_context(nc.psum_tensor("tp%d" % i, [128, 1024], BF16))
           for i in range(2)]
    pb = [sAB[0][:, 0:512], sAB[0][:, 512:1024],
          sAB[1][:, 0:512], sAB[1][:, 512:1024],
          pO_[:, 0:512], pO_[:, 512:1024]]

    dS = S(nc, es, "dS")
    dG = S(nc, es, "dG")
    qP = S(nc, es, "qP")
    qV = S(nc, es, "qV")
    qA = S(nc, es, "qA")
    qG = S(nc, es, "qG")
    BG = BankGate()

    P, V, A, G, SY = nc.tensor, nc.vector, nc.scalar, nc.gpsimd, nc.sync

    with nc.Block():
        qV.inc(V.memset(cEPS[:], EPS))
        qV.inc(V.memset(c30[:], 1e-30))
        for dst, src in [(c_idn, idn_d), (c_one2, ones2_d), (c_hs, hs_d),
                         (c_sel, sel_d), (c_tri, tri_d), (c_io8, io8_d)]:
            dS.inc(SY.dma_start(out=dst[:], in_=src), 16)
        for n in cols_d:
            dS.inc(SY.dma_start(out=c_col[n][:], in_=cols_d[n]), 16)
        dS.inc(SY.dma_start(
            out=rwst[:].rearrange("p (a e) -> p a e", a=8),
            in_=rwT.rearrange("(a p) e -> p a e", p=128)), 16)
        for ec in range(8):
            for hh in range(2):
                if ec + hh > 0:
                    SY.wait_ge(qV.h, qV.n)
                dS.inc(SY.dma_start(
                    out=sc[:],
                    in_=xT[ec * 128:(ec + 1) * 128, hh * 2048:(hh + 1) * 2048]), 16)
                V.wait_ge(dS.h, dS.n)
                qV.inc(V.tensor_copy(
                    out=xfm[:, ec * T + hh * 2048: ec * T + (hh + 1) * 2048],
                    in_=sc[:]))
        qV.inc(V.tensor_copy(out=rw_bf[:], in_=rwst[:]))

        aout_dma = {0: 0, 1: 0}
        for g in range(4):
            Fg = 256 * g

            def load_w(which):
                SY.wait_ge(qV.h, qV.n)
                for ec in range(8):
                    dS.inc(SY.dma_start(
                        out=wst[:, ec * 256:(ec + 1) * 256],
                        in_=wT[which][ec * 128:(ec + 1) * 128, Fg:Fg + 256]), 16)
                V.wait_ge(dS.h, dS.n)
                V.wait_ge(qP.h, qP.n)
                qV.inc(V.tensor_copy(out=wbf[:], in_=wst[:]))
                return qV.n

            w_rdy = load_w("wv")
            G.wait_ge(qP.h, qP.n)
            qG.inc(G.memset(
                vx[:].rearrange("p (k h o) -> p k h o", k=32, h=4)[:, :, :, 64:65],
                1.0))
            for tch in range(32):
                bank = tch % 2
                P.wait_ge(qV.h, w_rdy)
                BG.wait(P, bank)
                for ec in range(8):
                    mm = P.matmul(
                        out=pb[bank][:, 0:256],
                        lhsT=xfm[:, ec * T + tch * 128: ec * T + (tch + 1) * 128],
                        rhs=wbf[:, ec * 256:(ec + 1) * 256],
                        start=(ec == 0), stop=(ec == 7))
                qP.inc(mm)
                V.wait_ge(qP.h, qP.n)
                qV.inc(V.tensor_copy(
                    out=vx[:, tch * 260:(tch + 1) * 260]
                        .rearrange("p (h o) -> p h o", h=4)[:, :, 0:64],
                    in_=pb[bank][:, 0:256].rearrange("p (h o) -> p h o", h=4)))
                BG.set(bank, qV)

            for which, dst_t, gcol, bcol in (
                    ("wq", qh, c_col["qg_c"], c_col["qb_c"]),
                    ("wk", kh, c_col["kg_c"], c_col["kb_c"])):
                w_rdy = load_w(which)
                for c in range(2):
                    for hc in range(2):
                        for i in range(4):
                            tt = hc * 4 + i
                            bank = i % 2
                            P.wait_ge(qV.h, w_rdy)
                            BG.wait(P, bank)
                            for ec in range(8):
                                mm = P.matmul(
                                    out=pb[bank][:],
                                    lhsT=wbf[:, ec * 256 + c * 128:
                                             ec * 256 + (c + 1) * 128],
                                    rhs=xfm[:, ec * T + tt * 512:
                                            ec * T + (tt + 1) * 512],
                                    start=(ec == 0), stop=(ec == 7))
                            qP.inc(mm)
                            V.wait_ge(qP.h, qP.n)
                            qV.inc(V.tensor_copy(out=qf32[:, i * 512:(i + 1) * 512],
                                                 in_=pb[bank][:]))
                            qV.inc(V.tensor_copy(out=qbf_s[:], in_=pb[bank][:]))
                            BG.set(bank, qV)
                            A.wait_ge(qV.h, qV.n)
                            qA.inc(A.activation(out=q2bf_s[:], in_=qbf_s[:],
                                                func=AF.Square))
                            BG.wait(P, "st")
                            P.wait_ge(qA.h, qA.n)
                            mm = P.matmul(out=pb[2][0:8, :],
                                          lhsT=c_hs[:, i * 8:(i + 1) * 8],
                                          rhs=qbf_s[:],
                                          start=(i == 0), stop=(i == 3))
                            mm = P.matmul(out=pb[3][0:8, :],
                                          lhsT=c_hs[:, i * 8:(i + 1) * 8],
                                          rhs=q2bf_s[:],
                                          start=(i == 0), stop=(i == 3))
                            qP.inc(mm)
                        A.wait_ge(qP.h, qP.n)
                        qA.inc(A.mul(out=stA[:, 0:512], in_=pb[2][0:8, :],
                                     mul=1.0 / 64))
                        V.wait_ge(qA.h, qA.n)
                        qV.inc(V.tensor_tensor(out=stA[:, 512:1024],
                                               in0=stA[:, 0:512],
                                               in1=stA[:, 0:512], op=ALU.mult))
                        qV.inc(V.scalar_tensor_tensor(
                            out=stA[:, 512:1024], in0=pb[3][0:8, :],
                            scalar=1.0 / 64, in1=stA[:, 512:1024],
                            op0=ALU.mult, op1=ALU.subtract))
                        A.wait_ge(qV.h, qV.n)
                        qA.inc(A.activation(out=stA[:, 512:1024],
                                            in_=stA[:, 512:1024],
                                            func=AF.Ln, bias=cEPS[0:8, 0:1]))
                        qA.inc(A.activation(out=stA[:, 512:1024],
                                            in_=stA[:, 512:1024],
                                            func=AF.Exp, scale=-0.5))
                        V.wait_ge(qA.h, qA.n)
                        qV.inc(V.tensor_copy(out=stR[:], in_=stA[:, 512:1024]))
                        qV.inc(V.tensor_tensor(out=stMR[:], in0=stA[:, 0:512],
                                               in1=stA[:, 512:1024], op=ALU.mult))
                        BG.set("st", qA, qV)
                        st_rdy = qV.n
                        for i in range(4):
                            tt = hc * 4 + i
                            P.wait_ge(qV.h, max(st_rdy, qV.n))
                            BG.wait(P, 4)
                            BG.wait(P, 5)
                            mm = P.matmul(out=pb[4][:],
                                          lhsT=c_sel[:, i * 128:(i + 1) * 128],
                                          rhs=stR[:], start=True, stop=True)
                            mm = P.matmul(out=pb[5][:],
                                          lhsT=c_sel[:, i * 128:(i + 1) * 128],
                                          rhs=stMR[:], start=True, stop=True)
                            qP.inc(mm)
                            V.wait_ge(qP.h, qP.n)
                            qV.inc(V.tensor_tensor(
                                out=ap_s[:], in0=qf32[:, i * 512:(i + 1) * 512],
                                in1=pb[4][:], op=ALU.mult))
                            qV.inc(V.tensor_tensor(out=ap_s[:], in0=ap_s[:],
                                                   in1=pb[5][:], op=ALU.subtract))
                            qV.inc(V.tensor_scalar(
                                out=dst_t[:, c * T + tt * 512:
                                          c * T + (tt + 1) * 512],
                                in0=ap_s[:], scalar1=gcol[:, 0:1],
                                scalar2=bcol[:, 0:1], op0=ALU.mult, op1=ALU.add))
                            BG.set(4, qV)
                            BG.set(5, qV)

            for c in range(2):
                qV.inc(V.tensor_reduce(out=mx[:, c:c + 1],
                                       in_=qh[:, c * T:(c + 1) * T], axis=AX.X,
                                       op=ALU.max, apply_absolute_value=True))
                qV.inc(V.tensor_reduce(out=mx[:, 2 + c:3 + c],
                                       in_=kh[:, c * T:(c + 1) * T], axis=AX.X,
                                       op=ALU.max, apply_absolute_value=True))
            qV.inc(V.drain())
            G.wait_ge(qV.h, qV.n)
            for c in range(2):
                for half in range(2):
                    hl = 2 * c + half
                    qG.inc(G.tensor_reduce(
                        out=mxrow[0:1, hl:hl + 1],
                        in_=mx[half * 64:(half + 1) * 64, c:c + 1],
                        axis=AX.C, op=ALU.max))
                    qG.inc(G.tensor_reduce(
                        out=mxrow[0:1, 4 + hl:5 + hl],
                        in_=mx[half * 64:(half + 1) * 64, 2 + c:3 + c],
                        axis=AX.C, op=ALU.max))
            V.wait_ge(qG.h, qG.n)
            V.wait_ge(qA.h, qA.n)
            qV.inc(V.tensor_tensor(out=nbrow[:], in0=mxrow[0:1, 0:4],
                                   in1=mxrow[0:1, 4:8], op=ALU.mult))
            V.drain()
            qV.inc(V.reduce_max(out=nbrow[0:1, 0:1], in_=nbrow[:], axis=AX.X))
            V.drain()
            qV.inc(V.tensor_scalar(out=nbrow[0:1, 0:1], in0=nbrow[0:1, 0:1],
                                   scalar1=(-2.0 * ISC),
                                   scalar2=None, op0=ALU.mult))
            qV.inc(V.drain())
            P.wait_ge(qV.h, qV.n)
            BG.wait(P, 4)
            qP.inc(P.matmul(out=pb[4][:, 0:1], lhsT=c_one2[0:1, :],
                            rhs=nbrow[0:1, 0:1], start=True, stop=True))
            V.wait_ge(qP.h, qP.n)
            qV.inc(V.tensor_copy(out=negB[:, 0:1], in_=pb[4][:, 0:1]))
            qV.inc(V.drain())
            BG.set(4, qV)

            if debug and g == 0:
                G.wait_ge(qV.h, qV.n)
                G.wait_ge(qG.h, qG.n)
                dG.inc(G.dma_start(out=dbg_qh, in_=qh[:]), 16)
                dG.inc(G.dma_start(out=dbg_kh, in_=kh[:]), 16)
                dG.inc(G.dma_start(out=dbg_vx, in_=vx[:]), 16)
                dG.inc(G.dma_start(out=dbg_nb, in_=negB[:]), 16)
                dG.inc(G.dma_start(out=dbg_mx, in_=mx[:]), 16)
                dG.inc(G.dma_start(out=dbg_mxr, in_=mxrow[:]), 16)
                dG.inc(G.dma_start(out=dbg_nbr, in_=nbrow[:]), 16)
            P.wait_ge(qG.h, qG.n)
            for hp in range(2):
                c = hp
                for qt in range(8):
                    nkc = 4 * qt + 4
                    exp_ct = []
                    s_ct = []

                    def emit_S(kc):
                        par = kc % 2
                        BG.wait(P, "s%d" % par)
                        if kc >= 2:
                            P.wait_ge(qA.h, exp_ct[kc - 2])
                        for h in range(2):
                            mm = P.matmul(
                                out=sAB[par][:, h * 512:(h + 1) * 512],
                                lhsT=kh[h * 64:(h + 1) * 64,
                                        c * T + kc * 128: c * T + (kc + 1) * 128],
                                rhs=qh[h * 64:(h + 1) * 64,
                                       c * T + qt * 512: c * T + (qt + 1) * 512],
                                start=True, stop=True,
                                tile_position=(h * 64, 0))
                        s_ct.append(qP.inc(mm))

                    def emit_exp(kc):
                        par = kc % 2
                        r = kc - 4 * qt
                        diag = r >= 0
                        if diag:
                            V.wait_ge(qP.h, s_ct[kc])
                            qV.inc(V.tensor_tensor(
                                out=sAB[par][:]
                                    .rearrange("p (h q) -> p h q", h=2)
                                    [:, :, r * 128:(r + 1) * 128],
                                in0=sAB[par][:]
                                    .rearrange("p (h q) -> p h q", h=2)
                                    [:, :, r * 128:(r + 1) * 128],
                                in1=c_tri[:].rearrange("p (o q) -> p o q", o=1)
                                    .broadcast_to((128, 2, 128)),
                                op=ALU.add))
                            qV.inc(V.drain())
                        A.wait_ge(qP.h, s_ct[kc])
                        A.wait_ge(qV.h, qV.n)
                        qA.inc(A.activation(
                            out=Pbuf[:, par * 1024:(par + 1) * 1024],
                            in_=sAB[par][:],
                            func=AF.Exp, bias=negB[:, 0:1], scale=ISC))
                        exp_ct.append(qA.n)
                        BG.set("s%d" % par, qA)
                        if diag and r > 0:
                            G.wait_ge(qA.h, qA.n)
                            qG.inc(G.memset(
                                Pbuf[:, par * 1024: par * 1024 + 1024]
                                    .rearrange("p (h q) -> p h q", h=2)
                                    [:, :, 0:r * 128], 0.0))

                    def emit_O(kc):
                        par = kc % 2
                        r = kc - 4 * qt
                        P.wait_ge(qA.h, exp_ct[kc])
                        if r > 0:
                            P.wait_ge(qG.h, qG.n)
                        if kc == 0:
                            BG.wait(P, "o")
                        for h in range(2):
                            hl = 2 * hp + h
                            mm = P.matmul(
                                out=pO_[0:65, h * 512:(h + 1) * 512],
                                lhsT=vx[:, kc * 260 + hl * 65:
                                        kc * 260 + (hl + 1) * 65],
                                rhs=Pbuf[:, par * 1024 + h * 512:
                                         par * 1024 + (h + 1) * 512],
                                start=(kc == 0), stop=(kc == nkc - 1))
                        qP.inc(mm)

                    emit_S(0)
                    emit_exp(0)
                    for kc in range(1, nkc):
                        emit_S(kc)
                        emit_exp(kc)
                        emit_O(kc - 1)
                    emit_O(nkc - 1)

                    for h in range(2):
                        hl = 2 * hp + h
                        gh = 4 * g + hl
                        A.wait_ge(qP.h, qP.n)
                        qA.inc(A.activation(out=rd[64:65, :],
                                            in_=pO_[64:65, h * 512:(h + 1) * 512],
                                            func=AF.Ln, bias=c30[64:65, 0:1]))
                        A.drain()
                        qA.inc(A.activation(out=rd[64:65, :], in_=rd[64:65, :],
                                            func=AF.Exp, scale=-1.0))
                        qA.inc(A.drain())
                        P.wait_ge(qA.h, qA.n)
                        BG.wait(P, "s0")
                        qP.inc(P.matmul(out=sAB[0][0:64, 0:512],
                                        lhsT=c_one2[64:65, 0:64],
                                        rhs=rd[64:65, :], start=True, stop=True))
                        V.wait_ge(qP.h, qP.n)
                        V.wait_ge(dG.h, aout_dma[h])
                        qV.inc(V.tensor_copy(out=rb[:], in_=sAB[0][0:64, 0:512]))
                        qV.inc(V.tensor_tensor(
                            out=aout_s[:, h * 512:(h + 1) * 512],
                            in0=pO_[0:64, h * 512:(h + 1) * 512],
                            in1=rb[:], op=ALU.mult))
                        BG.set("s0", qV)
                        if h == 1:
                            BG.set("o", qV)
                        G.wait_ge(qV.h, qV.n)
                        aout_dma[h] = dG.inc(G.dma_start(
                            out=aout_h[gh * 64:(gh + 1) * 64,
                                       qt * 512:(qt + 1) * 512],
                            in_=aout_s[:, h * 512:(h + 1) * 512]), 16)

        attn_dG = dG.n

        for ec in range(8):
            SY.wait_ge(qV.h, qV.n)
            dS.inc(SY.dma_start(out=wst[:, 0:1024],
                                in_=wT["wo"][ec * 128:(ec + 1) * 128, :]), 16)
            V.wait_ge(dS.h, dS.n)
            V.wait_ge(qP.h, qP.n)
            qV.inc(V.tensor_copy(out=wo_bf[:, ec * 1024:(ec + 1) * 1024],
                                 in_=wst[:, 0:1024]))
        dS.inc(SY.dma_start(out=wst[0:1, 1024:2048], in_=bo_row), 16)
        P.wait_ge(dS.h, dS.n)
        P.wait_ge(qV.h, qV.n)
        BG.wait(P, 3)
        BG.wait(P, 4)
        qP.inc(P.matmul(out=pb[3][:], lhsT=c_one2[0:1, :],
                        rhs=wst[0:1, 1024:1536], start=True, stop=True))
        qP.inc(P.matmul(out=pb[4][:], lhsT=c_one2[0:1, :],
                        rhs=wst[0:1, 1536:2048], start=True, stop=True))
        V.wait_ge(qP.h, qP.n)
        qV.inc(V.tensor_copy(out=bo_b[:, 0:512], in_=pb[3][:]))
        qV.inc(V.tensor_copy(out=bo_b[:, 512:1024], in_=pb[4][:]))
        wo_rdy = qV.n

        SY.wait_ge(dG.h, attn_dG)
        y2_fin = []
        rd_fin = []
        x1s_dma = []
        for tc in range(32):
            par = tc % 2
            if tc >= 2:
                SY.wait_ge(qP.h, rd_fin[tc - 2])
                SY.wait_ge(qV.h, y2_fin[tc - 2])
            for ec in range(8):
                dS.inc(SY.dma_start(
                    out=aord[:, par * 1024 + ec * 128: par * 1024 + (ec + 1) * 128],
                    in_=aout_h[ec * 128:(ec + 1) * 128, tc * 128:(tc + 1) * 128]),
                    16)
            dS.inc(SY.dma_start(out=sc[:, par * 1024:(par + 1) * 1024],
                                in_=x_in[tc * 128:(tc + 1) * 128, :]), 16)
            ld_tc = dS.n
            P.wait_ge(dS.h, ld_tc)
            P.wait_ge(qV.h, wo_rdy if tc == 0 else y2_fin[tc - 1])
            for dh in range(2):
                for ec in range(8):
                    mm = P.matmul(
                        out=pb[dh][:],
                        lhsT=aord[:, par * 1024 + ec * 128:
                                  par * 1024 + (ec + 1) * 128],
                        rhs=wo_bf[:, ec * 1024 + dh * 512:
                                  ec * 1024 + (dh + 1) * 512],
                        start=(ec == 0), stop=(ec == 7))
            rd_fin.append(qP.inc(mm))
            V.wait_ge(qP.h, qP.n)
            for dh in range(2):
                qV.inc(V.tensor_tensor(
                    out=zt[:, dh * 512:(dh + 1) * 512], in0=pb[dh][:],
                    in1=sc[:, par * 1024 + dh * 512: par * 1024 + (dh + 1) * 512],
                    op=ALU.add))
            y2_fin.append(qV.n)
            qV.inc(V.tensor_tensor(out=zt[:], in0=zt[:], in1=bo_b[:], op=ALU.add))
            A.wait_ge(qV.h, qV.n)
            qA.inc(A.activation(out=Pbuf[:, 0:1024], in_=zt[:], func=AF.Square,
                                accum_out=lnst[:, 1:2]))
            qA.inc(A.drain())
            V.wait_ge(qA.h, qA.n)
            qV.inc(V.reduce_sum(out=lnst[:, 0:1], in_=zt[:], axis=AX.X))
            V.drain()
            qV.inc(V.tensor_scalar(out=lnst[:, 2:3], in0=lnst[:, 0:1],
                                   scalar1=1.0 / D, scalar2=None, op0=ALU.mult))
            V.drain()
            qV.inc(V.tensor_tensor(out=lnst[:, 3:4], in0=lnst[:, 2:3],
                                   in1=lnst[:, 2:3], op=ALU.mult))
            V.drain()
            qV.inc(V.scalar_tensor_tensor(out=lnst[:, 3:4], in0=lnst[:, 1:2],
                                          scalar=1.0 / D, in1=lnst[:, 3:4],
                                          op0=ALU.mult, op1=ALU.subtract))
            qV.inc(V.drain())
            A.wait_ge(qV.h, qV.n)
            qA.inc(A.activation(out=lnst[:, 3:4], in_=lnst[:, 3:4], func=AF.Ln,
                                bias=cEPS[:, 0:1]))
            A.drain()
            qA.inc(A.activation(out=lnst[:, 4:5], in_=lnst[:, 3:4], func=AF.Exp,
                                scale=-0.5))
            qA.inc(A.drain())
            V.wait_ge(qA.h, qA.n)
            if len(x1s_dma) >= 2:
                V.wait_ge(dG.h, x1s_dma[-2])
            qV.inc(V.tensor_scalar(out=x1f[:], in0=zt[:], scalar1=lnst[:, 2:3],
                                   scalar2=lnst[:, 4:5], op0=ALU.subtract,
                                   op1=ALU.mult))
            qV.inc(V.tensor_scalar(out=x1b[:], in0=zt[:], scalar1=lnst[:, 2:3],
                                   scalar2=lnst[:, 4:5], op0=ALU.subtract,
                                   op1=ALU.mult))
            x1_rdy = qV.n
            G.wait_ge(qV.h, x1_rdy)
            dG.inc(G.dma_start(out=x1_out[tc * 128:(tc + 1) * 128, :], in_=x1f[:]),
                   16)
            ev = []
            for j in range(8):
                P.wait_ge(qV.h, x1_rdy if j < 2 else ev[j - 2])
                qP.inc(P.transpose(out=tps[j % 2][:, 0:128],
                                   in_=x1b[:, j * 128:(j + 1) * 128],
                                   identity=c_idn[:]))
                V.wait_ge(qP.h, qP.n)
                ev.append(qV.inc(V.tensor_copy(
                    out=x1fmt[:, j * 128:(j + 1) * 128],
                    in_=tps[j % 2][:, 0:128])))
            qV.inc(V.drain())
            G.wait_ge(qV.h, qV.n)
            for j in range(8):
                dG.inc(G.dma_start(
                    out=x1fmT_out[j * 128:(j + 1) * 128, tc * 128:(tc + 1) * 128],
                    in_=x1fmt[:, j * 128:(j + 1) * 128]), 16)
            x1s_dma.append(dG.n)
            P.wait_ge(qV.h, qV.n)
            P.wait_ge(qA.h, qA.n)
            BG.wait(P, 2)
            for ec in range(8):
                mm = P.matmul(out=pb[2][:, 0:8],
                              lhsT=x1fmt[:, ec * 128:(ec + 1) * 128],
                              rhs=rw_bf[:, ec * 8:(ec + 1) * 8],
                              start=(ec == 0), stop=(ec == 7))
            qP.inc(mm)
            V.wait_ge(qP.h, qP.n)
            qV.inc(V.reduce_max(out=lnst[:, 5:6], in_=pb[2][:, 0:8], axis=AX.X))
            V.drain()
            qV.inc(V.tensor_scalar(out=lnst[:, 6:7], in0=lnst[:, 5:6],
                                   scalar1=-1.0, scalar2=None, op0=ALU.mult))
            qV.inc(V.drain())
            A.wait_ge(qV.h, qV.n)
            qA.inc(A.activation(out=e8[:], in_=pb[2][:, 0:8], func=AF.Exp,
                                bias=lnst[:, 6:7], accum_out=lnst[:, 7:8]))
            qA.inc(A.drain())
            V.wait_ge(qA.h, qA.n)
            qV.inc(V.reciprocal(out=route_p[:, tc:tc + 1], in_=lnst[:, 7:8]))
            qV.inc(V.tensor_scalar(out=oh8[:], in0=pb[2][:, 0:8],
                                   scalar1=lnst[:, 5:6], scalar2=None,
                                   op0=ALU.is_ge))
            V.drain()
            qV.inc(V.scalar_tensor_tensor(out=j8[:], in0=oh8[:], scalar=1.0,
                                          in1=c_io8[:], op0=ALU.mult,
                                          op1=ALU.mult,
                                          accum_out=route_i[:, tc:tc + 1]))
            BG.set(2, qV)
        qV.inc(V.drain())
        G.wait_ge(qV.h, qV.n)
        dG.inc(G.dma_start(
            out=route_out[0:1, :].rearrange("o (p c) -> (o p) c", p=128),
            in_=route_p[:]), 16)
        dG.inc(G.dma_start(
            out=route_out[1:2, :].rearrange("o (p c) -> (o p) c", p=128),
            in_=route_i[:]), 16)
        G.wait_ge(dG.h, dG.n)
        G.wait_ge(dS.h, dS.n)
    es.close()
    return nc


def build_l2():
    nc = bass.Bass("TRN2", target_bir_lowering=False, debug=False)
    dr = lambda n, s, d, k="ExternalInput": nc.dram_tensor(n, s, d, kind=k).ap()
    x1_d = dr("x1", [T, D], F32)
    x1fmT_d = dr("x1fmT", [D, T], BF16)
    myr_d = dr("my_route", [2, T], F32)
    allr_d = dr("routes_all", [2 * NE, T], F32)
    w1_d = dr("W1", [NE, D, DFF], F32)
    w2_d = dr("W2", [NE, DFF, D], F32)
    b1c_d = dr("b1cols", [NE, 128, 32], F32)
    b2r_d = dr("b2row", [1, NE * D], F32)
    onesb_d = dr("ones_b", [1, 128], BF16)
    onesc_d = dr("ones_c", [128, 1], F32)
    onef_d = dr("ones_f", [1, 128], F32)
    out_d = dr("out_g", [T, D], F32, "ExternalOutput")
    acc_h = nc.dram_tensor("acc_h", [T, D], BF16).ap()

    es = ExitStack()
    sb = lambda n, s, d: es.enter_context(nc.sbuf_tensor(n, s, d))
    w1bf = sb("w1bf", [128, 8 * DFF], BF16)
    w2bf = sb("w2bf", [128, 32 * D], BF16)
    wstg = sb("wstg", [128, 4096], F32)
    xblk = sb("xblk", [128, 2 * 8 * 384], BF16)
    hbf = sb("hbf", [128, 2 * 384], BF16)
    acci = sb("acci", [128, 2 * 3 * 1024], BF16)
    acco = sb("acco", [128, 2 * 3 * 1024], BF16)
    pall = sb("pall", [128, 256], F32)
    iall = sb("iall", [128, 256], F32)
    ploc = sb("ploc", [128, 32], F32)
    iloc = sb("iloc", [128, 32], F32)
    pm = sb("pm", [128, 2048], F32)
    ge = sb("ge", [128, 256], F32)
    ohb = sb("ohb", [128, 256], F32)
    cntp = sb("cntp", [128, 8], F32)
    masks = sb("masks", [128, 256], F32)
    kbuf = sb("kbuf", [128, 32], F32)
    lo = sb("lo", [1, 8], F32)
    hi = sb("hi", [1, 8], F32)
    midr = sb("midr", [1, 8], F32)
    cond = sb("cond", [1, 8], F32)
    mid_b = sb("mid_b", [128, 8], F32)
    tr8 = sb("tr8", [1, 8], F32)
    lo_b = sb("lo_b", [128, 8], F32)
    b1c = sb("b1c", [128, 32], F32)
    b2e = sb("b2e", [1, D], F32)
    b2be = sb("b2be", [1, D], BF16)
    zt = sb("zt", [128, D], F32)
    lnst = sb("lnst", [128, 8], F32)
    c_onesb = sb("c_onesb", [1, 128], BF16)
    c_onesc = sb("c_onesc", [128, 1], F32)
    c_onef = sb("c_onef", [1, 128], F32)
    cEPS = sb("cEPS", [128, 1], F32)

    pb = [es.enter_context(nc.psum_tensor("pb%d" % i, [128, 512], F32))
          for i in range(8)]

    dS = S(nc, es, "dS")
    dG = S(nc, es, "dG")
    qP = S(nc, es, "qP")
    qV = S(nc, es, "qV")
    qA = S(nc, es, "qA")

    P, V, A, G, SY = nc.tensor, nc.vector, nc.scalar, nc.gpsimd, nc.sync
    blk_off = [0]
    for b_ in BLKS:
        blk_off.append(blk_off[-1] + b_)

    with nc.Block():
        qV.inc(V.memset(cEPS[:], EPS))
        for dst, src in [(c_onesb, onesb_d), (c_onesc, onesc_d),
                         (c_onef, onef_d)]:
            dS.inc(SY.dma_start(out=dst[:], in_=src), 16)
        dS.inc(SY.dma_start(out=b2e[:], in_=b2r_d[0:1, 0:D]), 16)
        for a in range(8):
            dS.inc(SY.dma_start(
                out=pall[a * 16:(a + 1) * 16, :],
                in_=allr_d[2 * a:2 * a + 1, :]
                    .rearrange("o (b c) -> (o b) c", b=16)), 16)
            dS.inc(SY.dma_start(
                out=iall[a * 16:(a + 1) * 16, :],
                in_=allr_d[2 * a + 1:2 * a + 2, :]
                    .rearrange("o (b c) -> (o b) c", b=16)), 16)
        dS.inc(SY.dma_start(
            out=ploc[:], in_=myr_d[0:1, :].rearrange("o (p c) -> (o p) c", p=128)),
            16)
        dS.inc(SY.dma_start(
            out=iloc[:], in_=myr_d[1:2, :].rearrange("o (p c) -> (o p) c", p=128)),
            16)
        V.wait_ge(dS.h, dS.n)
        qV.inc(V.tensor_copy(out=b2be[:], in_=b2e[:]))
        b2_rdy = qV.n
        for e in range(NE):
            qV.inc(V.tensor_scalar(out=ohb[:], in0=iall[:], scalar1=float(e),
                                   scalar2=None, op0=ALU.is_equal))
            V.drain()
            qV.inc(V.tensor_tensor(out=pm[:, e * 256:(e + 1) * 256], in0=pall[:],
                                   in1=ohb[:], op=ALU.mult))
        qV.inc(V.memset(lo[:], 0.0))
        qV.inc(V.memset(hi[:], 1.0))
        qV.inc(V.drain())
        for it in range(NITER):
            qV.inc(V.tensor_tensor(out=midr[:], in0=lo[:], in1=hi[:], op=ALU.add))
            V.drain()
            qV.inc(V.tensor_scalar(out=midr[:], in0=midr[:], scalar1=0.5,
                                   scalar2=None, op0=ALU.mult))
            qV.inc(V.drain())
            P.wait_ge(qV.h, qV.n)
            qP.inc(P.matmul(out=pb[0][:, 0:8], lhsT=c_onef[:], rhs=midr[:],
                            start=True, stop=True))
            V.wait_ge(qP.h, qP.n)
            qV.inc(V.tensor_copy(out=mid_b[:], in_=pb[0][:, 0:8]))
            V.drain()
            for e in range(NE):
                qV.inc(V.tensor_scalar(out=ge[:], in0=pm[:, e * 256:(e + 1) * 256],
                                       scalar1=mid_b[:, e:e + 1], scalar2=None,
                                       op0=ALU.is_ge, op1=ALU.add,
                                       accum_out=cntp[:, e:e + 1]))
            qV.inc(V.drain())
            P.wait_ge(qV.h, qV.n)
            qP.inc(P.matmul(out=pb[1][0:1, 0:8], lhsT=c_onesc[:], rhs=cntp[:],
                            start=True, stop=True))
            V.wait_ge(qP.h, qP.n)
            qV.inc(V.tensor_scalar(out=cond[:], in0=pb[1][0:1, 0:8], scalar1=CAP,
                                   scalar2=None, op0=ALU.is_ge))
            V.drain()
            qV.inc(V.tensor_tensor(out=tr8[:], in0=midr[:], in1=lo[:],
                                   op=ALU.subtract))
            V.drain()
            qV.inc(V.tensor_tensor(out=tr8[:], in0=tr8[:], in1=cond[:],
                                   op=ALU.mult))
            V.drain()
            qV.inc(V.tensor_tensor(out=lo[:], in0=lo[:], in1=tr8[:], op=ALU.add))
            V.drain()
            qV.inc(V.tensor_tensor(out=tr8[:], in0=hi[:], in1=midr[:],
                                   op=ALU.subtract))
            V.drain()
            qV.inc(V.tensor_tensor(out=tr8[:], in0=tr8[:], in1=cond[:],
                                   op=ALU.mult))
            V.drain()
            qV.inc(V.tensor_tensor(out=hi[:], in0=midr[:], in1=tr8[:], op=ALU.add))
            V.drain()
        P.wait_ge(qV.h, qV.n)
        qP.inc(P.matmul(out=pb[0][:, 0:8], lhsT=c_onef[:], rhs=lo[:],
                        start=True, stop=True))
        V.wait_ge(qP.h, qP.n)
        qV.inc(V.tensor_copy(out=lo_b[:], in_=pb[0][:, 0:8]))
        V.drain()
        for e in range(NE):
            qV.inc(V.tensor_scalar(out=masks[:, e * 32:(e + 1) * 32], in0=ploc[:],
                                   scalar1=lo_b[:, e:e + 1], scalar2=None,
                                   op0=ALU.is_ge))
        for e in range(NE):
            qV.inc(V.tensor_scalar(out=ohb[:, e * 32:(e + 1) * 32], in0=iloc[:],
                                   scalar1=float(e), scalar2=None,
                                   op0=ALU.is_equal))
        qV.inc(V.drain())
        for e in range(NE):
            qV.inc(V.tensor_tensor(out=masks[:, e * 32:(e + 1) * 32],
                                   in0=masks[:, e * 32:(e + 1) * 32],
                                   in1=ohb[:, e * 32:(e + 1) * 32], op=ALU.mult))
        qV.inc(V.drain())
        qV.inc(V.tensor_copy(out=kbuf[:], in_=masks[:, 0:32]))
        for e in range(1, NE):
            V.drain()
            qV.inc(V.tensor_tensor(out=kbuf[:], in0=kbuf[:],
                                   in1=masks[:, e * 32:(e + 1) * 32],
                                   op=ALU.add))
        V.drain()
        qV.inc(V.tensor_scalar(out=kbuf[:], in0=kbuf[:], scalar1=-1.0,
                               scalar2=1.0, op0=ALU.mult, op1=ALU.add))
        V.drain()
        qV.inc(V.tensor_tensor(out=masks[:, 0:32], in0=masks[:, 0:32],
                               in1=kbuf[:], op=ALU.add))
        qV.inc(V.drain())
        masks_rdy = qV.n

        wcast = {0: 0, 1: 0}

        def load_w1(e, free_qp):
            k = 0
            for ec in range(8):
                for hh in range(2):
                    pi = k % 2
                    SY.wait_ge(qV.h, wcast[pi])
                    dS.inc(SY.dma_start(
                        out=wstg[:, pi * 2048:(pi + 1) * 2048],
                        in_=w1_d[e, ec * 128:(ec + 1) * 128,
                                 hh * 2048:(hh + 1) * 2048]), 16)
                    V.wait_ge(dS.h, dS.n)
                    V.wait_ge(qP.h, free_qp)
                    wcast[pi] = qV.inc(V.tensor_copy(
                        out=w1bf[:, ec * DFF + hh * 2048:
                                 ec * DFF + (hh + 1) * 2048],
                        in_=wstg[:, pi * 2048:(pi + 1) * 2048]))
                    k += 1
            return qV.n

        def load_w2(e, free_qp):
            k = 0
            for pc in range(8):
                for hh in range(2):
                    pi = k % 2
                    SY.wait_ge(qV.h, wcast[pi])
                    dS.inc(SY.dma_start(
                        out=wstg[:, pi * 2048:(pi + 1) * 2048]
                            .rearrange("p (c q) -> p c q", q=1024),
                        in_=w2_d[e, pc * 512:(pc + 1) * 512, :]
                            .rearrange("(c p) q -> p c q", p=128)
                            [:, hh * 2:(hh + 1) * 2, :]), 16)
                    V.wait_ge(dS.h, dS.n)
                    V.wait_ge(qP.h, free_qp)
                    wcast[pi] = qV.inc(V.tensor_copy(
                        out=w2bf[:, pc * 4096 + hh * 2048:
                                 pc * 4096 + (hh + 1) * 2048],
                        in_=wstg[:, pi * 2048:(pi + 1) * 2048]))
                    k += 1
            return qV.n

        w1_rdy = load_w1(0, 0)
        w2_rdy = load_w2(0, 0)
        dS.inc(SY.dma_start(out=b1c[:], in_=b1c_d[0]), 16)
        b1_rdy = dS.n
        h_last_of_e = 0
        stt_done = 0
        acc_wr = {}
        relu_ct = []
        xb_rdy_tab = {}
        h_done_tab = {}

        def load_xblk(e, bi):
            BLn = BLKS[bi]
            c0n = blk_off[bi]
            idxn = e * len(BLKS) + bi
            parn = idxn % 2
            if idxn >= 2:
                G.wait_ge(qP.h, h_done_tab.get(idxn - 2, 0))
            for ec in range(8):
                dG.inc(G.dma_start(
                    out=xblk[:, parn * 3072 + ec * 384:
                             parn * 3072 + ec * 384 + BLn],
                    in_=x1fmT_d[ec * 128:(ec + 1) * 128, c0n:c0n + BLn]), 16)
            xb_rdy_tab[idxn] = dG.n

        load_xblk(0, 0)
        for e in range(NE):
            for bi, BL in enumerate(BLKS):
                idx = e * len(BLKS) + bi
                c0 = blk_off[bi]
                par = idx % 2
                ntc = BL // 128
                if bi + 1 < len(BLKS):
                    load_xblk(e, bi + 1)
                elif e + 1 < NE:
                    load_xblk(e + 1, 0)
                xb_rdy = xb_rdy_tab[idx]
                if e > 0:
                    SY.wait_ge(dG.h, acc_wr[(e - 1) * len(BLKS) + bi])
                    SY.wait_ge(qV.h, stt_done)
                    dS.inc(SY.dma_start(
                        out=acci[:, par * 3072: par * 3072 + ntc * 1024]
                            .rearrange("p (c q) -> p c q", q=1024),
                        in_=acc_h[c0:c0 + BL, :]
                            .rearrange("(c p) q -> p c q", p=128)), 16)
                acc_rdy = dS.n
                relu_ct = []

                def emit_h(dffc):
                    hb = dffc % 2
                    if dffc == 0:
                        P.wait_ge(qV.h, w1_rdy)
                        P.wait_ge(dG.h, xb_rdy)
                        P.wait_ge(qV.h, stt_done)
                        P.wait_ge(dS.h, b1_rdy)
                        P.wait_ge(qA.h, qA.n)
                    if dffc >= 2:
                        P.wait_ge(qA.h, relu_ct[dffc - 2])
                    for ec in range(8):
                        mm = P.matmul(
                            out=pb[hb][:, 0:BL],
                            lhsT=w1bf[:, ec * DFF + dffc * 128:
                                      ec * DFF + (dffc + 1) * 128],
                            rhs=xblk[:, par * 3072 + ec * 384:
                                     par * 3072 + ec * 384 + BL],
                            start=(ec == 0), stop=(ec == 7))
                    qP.inc(mm)
                    A.wait_ge(qP.h, qP.n)
                    qA.inc(A.activation(
                        out=hbf[:, hb * 384: hb * 384 + BL],
                        in_=pb[hb][:, 0:BL], func=AF.Relu,
                        bias=b1c[:, dffc:dffc + 1]))
                    relu_ct.append(qA.n)

                def emit_y(dffc):
                    hb = dffc % 2
                    P.wait_ge(qA.h, relu_ct[dffc])
                    if dffc == 0:
                        P.wait_ge(qV.h, w2_rdy)
                    for tc3 in range(ntc):
                        for dh in range(2):
                            mm = P.matmul(
                                out=pb[2 + tc3 * 2 + dh][:],
                                lhsT=hbf[:, hb * 384 + tc3 * 128:
                                         hb * 384 + (tc3 + 1) * 128],
                                rhs=w2bf[:, dffc * D + dh * 512:
                                         dffc * D + (dh + 1) * 512],
                                start=(dffc == 0), stop=False)
                    qP.inc(mm)

                emit_h(0)
                for dffc in range(1, 32):
                    emit_h(dffc)
                    emit_y(dffc - 1)
                h_done_tab[idx] = qP.n
                emit_y(31)
                P.wait_ge(qV.h, b2_rdy)
                for tc3 in range(ntc):
                    for dh in range(2):
                        mm = P.matmul(
                            out=pb[2 + tc3 * 2 + dh][:], lhsT=c_onesb[:],
                            rhs=b2be[0:1, dh * 512:(dh + 1) * 512],
                            start=False, stop=True)
                qP.inc(mm)
                if e == 0 and bi == 0:
                    V.wait_ge(qV.h, masks_rdy)
                V.wait_ge(qP.h, qP.n)
                if e > 0:
                    V.wait_ge(dS.h, acc_rdy)
                if idx >= 2:
                    V.wait_ge(dG.h, acc_wr.get(idx - 2, 0))
                for tc3 in range(ntc):
                    tcg = c0 // 128 + tc3
                    for dh in range(2):
                        dst = acco[:, par * 3072 + tc3 * 1024 + dh * 512:
                                   par * 3072 + tc3 * 1024 + (dh + 1) * 512]
                        mcol = masks[:, e * 32 + tcg: e * 32 + tcg + 1]
                        if e == 0:
                            qV.inc(V.tensor_scalar(
                                out=dst, in0=pb[2 + tc3 * 2 + dh][:],
                                scalar1=mcol, scalar2=None, op0=ALU.mult))
                        else:
                            qV.inc(V.scalar_tensor_tensor(
                                out=dst, in0=pb[2 + tc3 * 2 + dh][:],
                                scalar=mcol,
                                in1=acci[:, par * 3072 + tc3 * 1024 + dh * 512:
                                         par * 3072 + tc3 * 1024 + (dh + 1) * 512],
                                op0=ALU.mult, op1=ALU.add))
                stt_done = qV.n
                G.wait_ge(qV.h, stt_done)
                acc_wr[idx] = dG.inc(G.dma_start(
                    out=acc_h[c0:c0 + BL, :].rearrange("(c p) q -> p c q", p=128),
                    in_=acco[:, par * 3072: par * 3072 + ntc * 1024]
                        .rearrange("p (c q) -> p c q", q=1024)), 16)
                if bi == len(BLKS) - 1:
                    h_last_of_e = qP.n
            if e + 1 < NE:
                w1_rdy = load_w1(e + 1, h_last_of_e)
                w2_rdy = load_w2(e + 1, qP.n)
                SY.wait_ge(qA.h, qA.n)
                dS.inc(SY.dma_start(out=b1c[:], in_=b1c_d[e + 1]), 16)
                b1_rdy = dS.n
                SY.wait_ge(qP.h, qP.n)
                dS.inc(SY.dma_start(out=b2e[:],
                                    in_=b2r_d[0:1, (e + 1) * D:(e + 2) * D]), 16)
                V.wait_ge(dS.h, dS.n)
                qV.inc(V.tensor_copy(out=b2be[:], in_=b2e[:]))
                b2_rdy = qV.n

        all_acc = dG.n
        ostore = []
        for tc in range(32):
            par = tc % 2
            if tc == 0:
                SY.wait_ge(dG.h, all_acc)
            if tc >= 2:
                SY.wait_ge(qV.h, qV.n)
            dS.inc(SY.dma_start(out=wstg[:, par * D:(par + 1) * D],
                                in_=x1_d[tc * 128:(tc + 1) * 128, :]), 16)
            dS.inc(SY.dma_start(out=acci[:, par * D:(par + 1) * D],
                                in_=acc_h[tc * 128:(tc + 1) * 128, :]), 16)
            V.wait_ge(dS.h, dS.n)
            qV.inc(V.tensor_tensor(out=zt[:], in0=wstg[:, par * D:(par + 1) * D],
                                   in1=acci[:, par * D:(par + 1) * D],
                                   op=ALU.add))
            A.wait_ge(qV.h, qV.n)
            qA.inc(A.activation(out=xblk[:, 0:1024], in_=zt[:], func=AF.Square,
                                accum_out=lnst[:, 1:2]))
            qA.inc(A.drain())
            V.wait_ge(qA.h, qA.n)
            qV.inc(V.reduce_sum(out=lnst[:, 0:1], in_=zt[:], axis=AX.X))
            V.drain()
            qV.inc(V.tensor_scalar(out=lnst[:, 2:3], in0=lnst[:, 0:1],
                                   scalar1=1.0 / D, scalar2=None, op0=ALU.mult))
            V.drain()
            qV.inc(V.tensor_tensor(out=lnst[:, 3:4], in0=lnst[:, 2:3],
                                   in1=lnst[:, 2:3], op=ALU.mult))
            V.drain()
            qV.inc(V.scalar_tensor_tensor(out=lnst[:, 3:4], in0=lnst[:, 1:2],
                                          scalar=1.0 / D, in1=lnst[:, 3:4],
                                          op0=ALU.mult, op1=ALU.subtract))
            qV.inc(V.drain())
            A.wait_ge(qV.h, qV.n)
            qA.inc(A.activation(out=lnst[:, 3:4], in_=lnst[:, 3:4], func=AF.Ln,
                                bias=cEPS[:, 0:1]))
            A.drain()
            qA.inc(A.activation(out=lnst[:, 4:5], in_=lnst[:, 3:4], func=AF.Exp,
                                scale=-0.5))
            qA.inc(A.drain())
            V.wait_ge(qA.h, qA.n)
            if len(ostore) >= 2:
                V.wait_ge(dG.h, ostore[-2])
            qV.inc(V.tensor_scalar(out=pm[:, par * D:(par + 1) * D], in0=zt[:],
                                   scalar1=lnst[:, 2:3], scalar2=lnst[:, 4:5],
                                   op0=ALU.subtract, op1=ALU.mult))
            G.wait_ge(qV.h, qV.n)
            ostore.append(dG.inc(G.dma_start(
                out=out_d[tc * 128:(tc + 1) * 128, :],
                in_=pm[:, par * D:(par + 1) * D]), 16))
        G.wait_ge(dG.h, dG.n)
        G.wait_ge(dS.h, dS.n)
    es.close()
    return nc


_NC_CACHE = {}


def _get_nc(which):
    if which not in _NC_CACHE:
        _NC_CACHE[which] = build_l1() if which == "l1" else build_l2()
    return _NC_CACHE[which]


def kernel(L0_attn_x, L0_attn_Wk, L0_attn_Wq, L0_attn_Wv, L0_attn_Wo,
           L0_attn_bo, L0_attn_kln_g, L0_attn_kln_b, L0_attn_qln_g,
           L0_attn_qln_b, L0_ffn_router_w, L0_ffn_W1, L0_ffn_b1, L0_ffn_W2,
           L0_ffn_b2):
    x = np.ascontiguousarray(np.asarray(L0_attn_x, dtype=np.float32))
    cst = host_consts()
    C = np.ascontiguousarray

    shared1 = {
        "wqT": C(np.asarray(L0_attn_Wq, np.float32).T),
        "wkT": C(np.asarray(L0_attn_Wk, np.float32).T),
        "wvT": C(np.asarray(L0_attn_Wv, np.float32).T),
        "woT": C(np.asarray(L0_attn_Wo, np.float32).T),
        "rwT": C(np.asarray(L0_ffn_router_w, np.float32).T),
        "bo_row": np.asarray(L0_attn_bo, np.float32).reshape(1, D),
        "qg_c": C(np.tile(np.asarray(L0_attn_qln_g, np.float32), 2)).reshape(128, 1),
        "qb_c": C(np.tile(np.asarray(L0_attn_qln_b, np.float32), 2)).reshape(128, 1),
        "kg_c": C(np.tile(np.asarray(L0_attn_kln_g, np.float32), 2)).reshape(128, 1),
        "kb_c": C(np.tile(np.asarray(L0_attn_kln_b, np.float32), 2)).reshape(128, 1),
        "idn_bf": cst["idn_bf"], "ones2": cst["ones2"], "hs_bf": cst["hs_bf"],
        "sel_bf": cst["sel_bf"], "tri": cst["tri"], "iota8": cst["iota8"],
    }
    in_maps1 = []
    for c in range(NCORE):
        m = dict(shared1)
        m["x"] = x[c]
        m["xT"] = C(x[c].T)
        in_maps1.append(m)

    nc1 = _get_nc("l1")
    r1 = run_bass_kernel_spmd(nc1, in_maps1, core_ids=list(range(NCORE)))

    routes_all = C(np.concatenate(
        [np.asarray(r1.results[c]["route_out"]) for c in range(NCORE)],
        axis=0).astype(np.float32))

    b1cols = C(np.asarray(L0_ffn_b1, np.float32).reshape(NE, 32, 128)
               .transpose(0, 2, 1))
    shared2 = {
        "routes_all": routes_all,
        "W1": np.asarray(L0_ffn_W1, np.float32),
        "W2": np.asarray(L0_ffn_W2, np.float32),
        "b1cols": b1cols,
        "b2row": np.asarray(L0_ffn_b2, np.float32).reshape(1, NE * D),
        "ones_b": cst["ones_b"], "ones_c": cst["ones_c"],
        "ones_f": cst["ones_f"],
    }
    in_maps2 = []
    for c in range(NCORE):
        m = dict(shared2)
        m["x1"] = np.asarray(r1.results[c]["x1_out"])
        m["x1fmT"] = np.asarray(r1.results[c]["x1fmT_out"])
        m["my_route"] = np.asarray(r1.results[c]["route_out"])
        in_maps2.append(m)

    nc2 = _get_nc("l2")
    r2 = run_bass_kernel_spmd(nc2, in_maps2, core_ids=list(range(NCORE)))

    out = np.stack([np.asarray(r2.results[c]["out_g"]) for c in range(NCORE)],
                   axis=0)
    return out.astype(np.float32)


if __name__ == "__main__":
    print("building l1...")
    build_l1()
    print("l1 ok; building l2...")
    build_l2()
    print("l2 ok")
